# version 04 (final)
# speedup vs baseline: 1.1163x; 1.0116x over previous
import numpy as np
import ml_dtypes
import concourse.bass as bass
import concourse.mybir as mybir
from concourse.bass_utils import run_bass_kernel_spmd

F32 = mybir.dt.float32
BF16 = mybir.dt.bfloat16
AF = mybir.ActivationFunctionType
ALU = mybir.AluOpType
AX = mybir.AxisListType

BF = ml_dtypes.bfloat16
NCORES = 8
T_CORE = 1024
NTILE = 8
D = 2048
NDC = 16

ENGINES = ("sync", "scalar", "vector", "tensor", "gpsimd")


_SCOPE = [None]


def SB(nc, name, shape, dtype, **kw):
    if _SCOPE[0] is None:
        return nc.alloc_sbuf_tensor(name, shape, dtype, **kw)
    return _SCOPE[0].enter_context(nc.sbuf_tensor(name, shape, dtype, **kw))


def PS(nc, name, shape, dtype, **kw):
    if _SCOPE[0] is None:
        return nc.alloc_psum_tensor(name, shape, dtype, **kw)
    return _SCOPE[0].enter_context(nc.psum_tensor(name, shape, dtype, **kw))


class Sem:
    __slots__ = ("h", "count")
    def __init__(self, h):
        self.h = h
        self.count = 0


class Prog:
    def __init__(self, nc):
        self.nc = nc
        self.streams = {e: [] for e in ENGINES}
        self._semi = 0

    def sem(self, name=None):
        self._semi += 1
        if not hasattr(self, "_sempool"):
            self._sempool = []
        if self._semi <= len(self._sempool):
            return self._sempool[self._semi - 1]
        s = Sem(self.nc.alloc_semaphore(name or f"s{self._semi}"))
        self._sempool.append(s)
        return s

    def sem_swdge(self, name=None):
        if not hasattr(self, "_swpool"):
            self._swpool = []
        if not hasattr(self, "_swsemi"):
            self._swsemi = 0
        self._swsemi += 1
        if self._swsemi <= len(self._swpool):
            return self._swpool[self._swsemi - 1]
        s = Sem(self.nc.alloc_semaphore(name or f"sw{self._swsemi}"))
        self._swpool.append(s)
        return s

    def sem_scope_reset(self):
        self._semi = 0
        self._swsemi = 0
        if not hasattr(self, "_swpool"):
            self._swpool = []

    def barrier(self):
        if not hasattr(self, "_sbar"):
            self._sbar = Sem(self.nc.alloc_semaphore("sbarrier"))
        s = self._sbar
        for eng in ENGINES:
            self.emit(eng, lambda e: e.nop(), s)
        tgt = s.count
        for eng in ENGINES:
            self.wait(eng, s, tgt)

    def on(self, eng, fn):
        self.streams[eng].append(fn)

    def wait(self, eng, sem, val=None):
        v = sem.count if val is None else val
        if v > 0:
            self.on(eng, lambda e, h=sem.h, v=v: e.wait_ge(h, v))

    def emit(self, eng, fn, sem=None, inc=1):
        if sem is not None:
            sem.count += inc
            self.on(eng, lambda e, fn=fn, h=sem.h, k=inc: fn(e).then_inc(h, k))
            return sem.count
        self.on(eng, fn)
        return None

    def dma(self, eng, out, in_, sem, **kw):
        sem.count += 16
        self.on(eng, lambda e, o=out, i=in_, h=sem.h, kw=kw:
                e.dma_start(out=o, in_=i, **kw).then_inc(h, 16))
        return sem.count

    def dmat(self, eng, out, in_, sem):
        sem.count += 16
        self.on(eng, lambda e, o=out, i=in_, h=sem.h:
                e.dma_start_transpose(out=o, in_=i).then_inc(h, 16))
        return sem.count

    def build_block(self):
        with self.nc.Block() as blk:
            for eng in ENGINES:
                fns = self.streams[eng]
                if not fns:
                    continue
                def runner(e, fns=fns):
                    for fn in fns:
                        fn(e)
                getattr(blk, eng)(runner)


class Stream:
    def __init__(self, p, nbuf=2, swdge=False):
        self.p = p
        self.sems = [(p.sem_swdge() if swdge else p.sem()) for _ in range(nbuf)]
        self.i = 0

    def dma(self, eng, out, in_, gate=None, slot=None, **kw):
        if slot is None:
            slot = self.i % len(self.sems)
            self.i += 1
        s = self.sems[slot]
        if gate is not None:
            gates = gate if isinstance(gate, list) else [gate]
            for g in gates:
                self.p.wait(eng, g[0], g[1])
        self.p.dma(eng, out, in_, s, **kw)
        return (s, s.count)


class TP:
    _count = [0]

    def __init__(self, p, ident_sb, nbanks=2):
        TP._count[0] += 1
        self._tpid = TP._count[0]
        self.p = p
        self.ident = ident_sb
        self.banks = [PS(p.nc, f"ptp{self._tpid}_{i}", [128, 512], BF16)
                      for i in range(nbanks)]
        self.rel = [0] * nbanks
        self.i = 0
        self.s_tp = p.sem("s_tp")
        self.s_ev = p.sem("s_ev")

    def group(self, srcs, dsts, src_ready=None):
        p = self.p
        b = self.i % len(self.banks)
        self.i += 1
        p.wait("tensor", self.s_ev, self.rel[b])
        if src_ready is not None:
            p.wait("tensor", src_ready[0], src_ready[1])
        bank = self.banks[b]
        n = len(srcs)
        for j, src in enumerate(srcs):
            w = src.shape[0]
            p.emit("tensor",
                   lambda e, o=bank[0:src.shape[1], j * 128: j * 128 + w], i=src,
                   idn=self.ident[0:w, 0:w]: e.transpose(o, i, idn),
                   self.s_tp if j == n - 1 else None)
        p.wait("vector", self.s_tp, self.s_tp.count)
        for j, (s, d) in enumerate(zip(srcs, dsts)):
            w = s.shape[0]
            p.emit("vector",
                   lambda e, o=d, i=bank[:, j * 128: j * 128 + w]: e.tensor_copy(o, i),
                   self.s_ev if j == n - 1 else None)
        self.rel[b] = self.s_ev.count


def core_strips(core):
    return (core, 15 - core)


class LNBufs:
    def __init__(self, nc, tag):
        self.x = SB(nc, f"lnx{tag}", [128, 2 * 2048], F32)
        self.stats = SB(nc, f"lnst{tag}", [128, 24], F32)
        self.mv = SB(nc, f"lnmv{tag}", [128, 2], F32)
        self.rstd = SB(nc, f"lnrs{tag}", [128, 1], F32)
        self.tmp = SB(nc, f"lntp{tag}", [128, 1], F32)


def emit_ln(p, lb, x_dram, s_x, s_v, s_sc, u_tok, tp=None, u_fm=None,
            u_tiles=NTILE):
    xs = Stream(p, 2)
    xrel = {0: 0, 1: 0}
    urel = {0: 0, 1: 0}
    for t in range(NTILE):
        buf = t % 2
        xv = lb.x[:, buf * 2048:(buf + 1) * 2048]
        xw = xs.dma("sync", xv, x_dram[t * 128:(t + 1) * 128, :],
                    gate=(s_v, xrel[buf]), slot=buf)
        p.wait("vector", xw[0], xw[1])
        for i in range(3):
            p.emit("vector", lambda e, o=lb.stats[:, i * 6:(i + 1) * 6],
                   x=xv[:, i * 512:(i + 1) * 512]: e.bn_stats(o, x))
        p.emit("vector", lambda e, o=lb.stats[:, 18:24], x=xv[:, 1536:2048]:
               e.bn_stats(o, x), s_v)
        p.wait("vector", s_v)
        p.emit("vector", lambda e, o=lb.mv, i=lb.stats: e.bn_aggr(o[:], i[:]), s_v)
        p.wait("vector", s_v)
        p.emit("vector", lambda e, o=lb.tmp, i=lb.mv: e.tensor_scalar(
            out=o[:], in0=i[:, 1:2], scalar1=1e-5, scalar2=None,
            op0=ALU.add, op1=ALU.bypass), s_v)
        p.wait("vector", s_v)
        p.emit("vector", lambda e, o=lb.rstd, i=lb.tmp: e.reciprocal(o[:], i[:]), s_v)
        p.wait("scalar", s_v, s_v.count)
        p.emit("scalar", lambda e, o=lb.rstd: e.activation(
            out=o[:], in_=o[:], func=AF.Sqrt), s_sc)
        p.wait("vector", s_sc, s_sc.count)
        ut = t % u_tiles
        uv = u_tok[:, ut * 2048:(ut + 1) * 2048]
        if u_tiles == 2:
            p.wait("vector", tp.s_tp, urel[ut])
        u_done = p.emit("vector", lambda e, o=uv, x=xv, m=lb.mv, r=lb.rstd:
                        e.tensor_scalar(out=o, in0=x, scalar1=m[:, 0:1], scalar2=r[:],
                                        op0=ALU.subtract, op1=ALU.mult), s_v)
        xrel[buf] = u_done
        if tp is not None:
            for g in range(4):
                srcs = [uv[:, (g * 4 + j) * 128:(g * 4 + j + 1) * 128] for j in range(4)]
                dsts = [u_fm[:, (g * 4 + j) * 1024 + t * 128:
                             (g * 4 + j) * 1024 + (t + 1) * 128] for j in range(4)]
                tp.group(srcs, dsts, src_ready=(s_v, u_done))
            urel[ut] = tp.s_tp.count


def emit_ffn0(p, nc, x_dram, xo_dram, wg, wu, wd, ident_sb, tag, coef=0.5):
    s_x = p.sem(); s_v = p.sem(); s_sc = p.sem()
    s_mm = p.sem(); s_out = p.sem()
    wn_s = Stream(p, 2, swdge=True)
    wd_s = Stream(p, 2, swdge=True)
    tp = TP(p, ident_sb)
    lb = LNBufs(nc, tag)
    u_tok = SB(nc, f"utok{tag}", [128, 2 * 2048], BF16)
    u_fm = SB(nc, f"ufm{tag}", [128, 16 * 1024], BF16)
    wnat = SB(nc, f"wnat{tag}", [128, 4 * 2048], BF16)
    wfm = SB(nc, f"wfm{tag}", [128, 4 * 2048], BF16)
    silu_sb = SB(nc, f"silu{tag}", [128, 2 * 512], BF16)
    gated = SB(nc, f"gated{tag}", [128, 8 * 1024], BF16)
    wdnat = SB(nc, f"wdnat{tag}", [128, 2 * 4 * 1024], BF16)
    wdfm = SB(nc, f"wdfm{tag}", [128, 2 * 8 * 512], BF16)
    acc = SB(nc, f"acc{tag}", [128, 8 * 2048], F32)

    pga = [PS(nc, f"pga{tag}{i}", [128, 512], F32) for i in range(4)]
    pdn = [PS(nc, f"pdn{tag}{i}", [128, 512], F32) for i in range(2)]
    pga_rel = [0] * 4
    pdn_rel = [0] * 2

    emit_ln(p, lb, x_dram, s_x, s_v, s_sc, u_tok, tp=tp, u_fm=u_fm, u_tiles=2)
    u_fm_ready = tp.s_ev.count

    wnat_rel = {}
    wfm_rel = {}
    wdnat_rel = {}
    wdfm_rel = {}

    def dma_w(w, g):
        buf = g % 2
        src = (wg if w == 0 else wu)[g * 128:(g + 1) * 128, :]
        dst = wnat[:, (w * 2 + buf) * 2048:(w * 2 + buf + 1) * 2048]
        return wn_s.dma("gpsimd", dst, src,
                        gate=(tp.s_tp, wnat_rel.get((w, buf), 0)), slot=buf)

    wn_after = {}
    for g in (0, 1):
        for w in (0, 1):
            wn_after[g] = dma_w(w, g)

    gated_rel = 0
    ga_i = 0

    for fb in range(8):
        for fc in range(8):
            g = fb * 8 + fc
            buf = g % 2
            for w in (0, 1):
                nat = wnat[:, (w * 2 + buf) * 2048:(w * 2 + buf + 1) * 2048]
                fm = wfm[:, (w * 2 + buf) * 2048:(w * 2 + buf + 1) * 2048]
                p.wait("tensor", s_mm, wfm_rel.get((w, buf), 0))
                for grp in range(4):
                    srcs = [nat[:, (grp * 4 + j) * 128:(grp * 4 + j + 1) * 128]
                            for j in range(4)]
                    dsts = [fm[:, (grp * 4 + j) * 128:(grp * 4 + j + 1) * 128]
                            for j in range(4)]
                    tp.group(srcs, dsts, src_ready=wn_after[g])
                wnat_rel[(w, buf)] = tp.s_tp.count
            wfm_ready = tp.s_ev.count
            if g + 2 < 64:
                for w in (0, 1):
                    wn_after[g + 2] = dma_w(w, g + 2)
            p.wait("tensor", tp.s_ev, max(wfm_ready, u_fm_ready))
            for th in range(2):
                bank_g = pga[ga_i % 4]; ga_i += 1
                bank_u = pga[ga_i % 4]; ga_i += 1
                p.wait("tensor", s_sc, pga_rel[pga.index(bank_g)])
                for dc in range(NDC):
                    lhs = wfm[:, (0 * 2 + buf) * 2048 + dc * 128:
                              (0 * 2 + buf) * 2048 + (dc + 1) * 128]
                    rhs = u_fm[:, dc * 1024 + th * 512: dc * 1024 + (th + 1) * 512]
                    p.emit("tensor", lambda e, o=bank_g, l=lhs, r=rhs, s=(dc == 0),
                           t2=(dc == NDC - 1): e.matmul(o[:], l, r, start=s, stop=t2),
                           s_mm if dc == NDC - 1 else None)
                gate_done = s_mm.count
                p.wait("tensor", s_v, pga_rel[pga.index(bank_u)])
                for dc in range(NDC):
                    lhs = wfm[:, (1 * 2 + buf) * 2048 + dc * 128:
                              (1 * 2 + buf) * 2048 + (dc + 1) * 128]
                    rhs = u_fm[:, dc * 1024 + th * 512: dc * 1024 + (th + 1) * 512]
                    p.emit("tensor", lambda e, o=bank_u, l=lhs, r=rhs, s=(dc == 0),
                           t2=(dc == NDC - 1): e.matmul(o[:], l, r, start=s, stop=t2),
                           s_mm if dc == NDC - 1 else None)
                up_done = s_mm.count
                p.wait("scalar", s_mm, gate_done)
                sl = silu_sb[:, th * 512:(th + 1) * 512]
                silu_done = p.emit("scalar", lambda e, o=sl, i=bank_g:
                                   e.activation(out=o, in_=i[:], func=AF.Silu), s_sc)
                pga_rel[pga.index(bank_g)] = silu_done
                p.wait("vector", s_mm, up_done)
                p.wait("vector", s_sc, silu_done)
                if fc == 0 and th == 0:
                    p.wait("vector", s_mm, gated_rel)
                gv = gated[:, fc * 1024 + th * 512: fc * 1024 + (th + 1) * 512]
                gdone = p.emit("vector", lambda e, o=gv, a=sl, b=bank_u:
                               e.tensor_tensor(out=o, in0=a, in1=b[:], op=ALU.mult), s_v)
                pga_rel[pga.index(bank_u)] = gdone
        gated_ready = s_v.count

        for dc4 in range(4):
            buf = dc4 % 2
            for ds in range(4):
                src = wd[dc4 * 512 + ds * 128: dc4 * 512 + (ds + 1) * 128,
                         fb * 1024:(fb + 1) * 1024]
                dst = wdnat[:, (buf * 4 + ds) * 1024:(buf * 4 + ds + 1) * 1024]
                wd_loaded = wd_s.dma(
                    "gpsimd", dst, src, slot=buf,
                    gate=(tp.s_tp, wdnat_rel.get(buf, 0)) if ds == 0 else None)
            p.wait("tensor", s_mm, wdfm_rel.get(buf, 0))
            for fc in range(8):
                srcs = [wdnat[:, (buf * 4 + ds) * 1024 + fc * 128:
                              (buf * 4 + ds) * 1024 + (fc + 1) * 128] for ds in range(4)]
                dsts = [wdfm[:, buf * 4096 + fc * 512 + ds * 128:
                             buf * 4096 + fc * 512 + (ds + 1) * 128] for ds in range(4)]
                tp.group(srcs, dsts, src_ready=wd_loaded)
            wdnat_rel[buf] = tp.s_tp.count
            wdfm_ready = tp.s_ev.count
            p.wait("tensor", tp.s_ev, wdfm_ready)
            p.wait("tensor", s_v, gated_ready)
            for tt in range(8):
                bank = pdn[tt % 2]
                p.wait("tensor", s_v, pdn_rel[tt % 2])
                for fc in range(8):
                    lhs = gated[:, fc * 1024 + tt * 128: fc * 1024 + (tt + 1) * 128]
                    rhs = wdfm[:, buf * 4096 + fc * 512: buf * 4096 + (fc + 1) * 512]
                    p.emit("tensor", lambda e, o=bank, l=lhs, r=rhs, s=(fc == 0),
                           t2=(fc == 7): e.matmul(o[:], l, r, start=s, stop=t2),
                           s_mm if fc == 7 else None)
                dn_done = s_mm.count
                wdfm_rel[buf] = dn_done
                p.wait("vector", s_mm, dn_done)
                av = acc[:, tt * 2048 + dc4 * 512: tt * 2048 + (dc4 + 1) * 512]
                if fb == 0:
                    rel = p.emit("vector", lambda e, o=av, i=bank:
                                 e.tensor_copy(o, i[:]), s_v)
                else:
                    rel = p.emit("vector", lambda e, o=av, i=bank:
                                 e.tensor_tensor(out=o, in0=o, in1=i[:], op=ALU.add), s_v)
                pdn_rel[tt % 2] = rel
        gated_rel = s_mm.count

    emit_residual_out(p, lb, acc, x_dram, xo_dram, coef, s_x, s_v, s_out)


def emit_residual_out(p, lb, acc, x_dram, xo_dram, coef, s_x, s_v, s_out):
    xs = Stream(p, 2)
    os_ = Stream(p, 2)
    out_rel = {0: (None, 0), 1: (None, 0)}
    for t in range(NTILE):
        buf = t % 2
        xv = lb.x[:, buf * 2048:(buf + 1) * 2048]
        gates = [(s_v, s_v.count)]
        if out_rel[buf][0] is not None:
            gates.append(out_rel[buf])
        xw = xs.dma("sync", xv, x_dram[t * 128:(t + 1) * 128, :],
                    gate=gates, slot=buf)
        p.wait("vector", xw[0], xw[1])
        p.emit("vector", lambda e, o=xv, a=acc[:, t * 2048:(t + 1) * 2048], x=xv, c=coef:
               e.scalar_tensor_tensor(out=o, in0=a, scalar=c, in1=x,
                                      op0=ALU.mult, op1=ALU.add), s_v)
        out_rel[buf] = os_.dma("scalar", xo_dram[t * 128:(t + 1) * 128, :], xv,
                               slot=buf, gate=(s_v, s_v.count))
    for s_ in os_.sems:
        if s_.count:
            p.wait("scalar", s_, s_.count)


def emit_moe(p, nc, x_dram, xo_dram, W, ident_sb, ident32_sb, tag, coef=0.5, dbg=None):
    s_x = p.sem(); s_v = p.sem(); s_sc = p.sem()
    s_mm = p.sem(); s_out = p.sem()
    s_cw = p.sem_swdge()
    w1_s = Stream(p, 2, swdge=True)
    w2_s = Stream(p, 1, swdge=True)
    b2_s = Stream(p, 1, swdge=True)
    tp = TP(p, ident_sb)
    lb = LNBufs(nc, tag)
    u_tok = SB(nc, f"utok{tag}", [128, 2 * 2048], BF16)
    u_fm = SB(nc, f"ufm{tag}", [128, 16 * 1024], BF16)
    acc = SB(nc, f"acc{tag}", [128, 8 * 2048], F32)
    h_sb = SB(nc, f"h{tag}", [128, 11 * 1024], BF16)
    w2_sb = SB(nc, f"w2{tag}", [128, 11 * 2048], BF16)
    w1col = SB(nc, f"w1c{tag}", [128, 2 * 2048], BF16)
    b1cols = SB(nc, f"b1c{tag}", [128, 11 * 18], F32)
    b2buf = SB(nc, f"b2b{tag}", [1, 2048], BF16)
    b1nat = SB(nc, f"b1n{tag}", [18, 1408], F32)
    _ut_addr = nc.lookup_mls(u_tok).memorylocations[0].addr
    centnat = nc.alloc_sbuf_tensor_at(f"cn{tag}", [16, 2048], BF16, offset=_ut_addr)
    cent_fm = SB(nc, f"cf{tag}", [128, 16 * 16], BF16)
    biasb = SB(nc, f"bb{tag}", [128, 16], F32)
    bias1 = SB(nc, f"b1r{tag}", [1, 16], F32)
    ones1 = SB(nc, f"on{tag}", [1, 128], BF16)
    ones1f = SB(nc, f"onf{tag}", [1, 128], F32)
    onescol = SB(nc, f"oc{tag}", [128, 1], F32)
    g_sb = SB(nc, f"g{tag}", [128, 8 * 16], F32)
    sc_sb = SB(nc, f"sc{tag}", [128, 16], F32)
    m_sb = SB(nc, f"m{tag}", [128, 1], F32)
    msk_sb = SB(nc, f"mk{tag}", [128, 16], F32)
    sum_sb = SB(nc, f"su{tag}", [128, 1], F32)

    p_a = [PS(nc, f"pa{tag}{i}", [128, 512], F32) for i in range(2)]
    p_b = [PS(nc, f"pb{tag}{i}", [128, 512], F32) for i in range(2)]
    p_t32 = PS(nc, f"pt32{tag}", [128, 128], F32)
    pa_rel = [0, 0]
    pb_rel = [0, 0]

    p.emit("vector", lambda e: e.memset(ones1[:], 1.0), s_v)
    p.emit("vector", lambda e: e.memset(ones1f[:], 1.0), s_v)
    p.emit("vector", lambda e: e.memset(onescol[:], 1.0), s_v)
    const_ready = s_v.count
    p.dma("gpsimd", b1nat[0:2, :], W["sb1"][:], s_cw)
    p.dma("gpsimd", b1nat[2:18, :], W["rb1"][:], s_cw)
    p.dma("gpsimd", centnat[:], W["cent"][:], s_cw)
    p.dma("sync", bias1[:], W["bias"][None, :], s_x)
    b_loaded = s_cw.count
    bias_loaded = s_x.count

    p.wait("tensor", s_cw, b_loaded)
    t32_rel = 0
    for fc in range(11):
        p.wait("tensor", s_v, t32_rel)
        p.emit("tensor", lambda e, i=b1nat[0:18, fc * 128:(fc + 1) * 128],
               o=p_t32[0:128, 0:18], idn=ident32_sb:
               e.transpose(o, i, idn[0:18, 0:18]), s_mm)
        p.wait("vector", s_mm, s_mm.count)
        t32_rel = p.emit("vector", lambda e, o=b1cols[:, fc * 18:(fc + 1) * 18],
                         i=p_t32[0:128, 0:18]: e.tensor_copy(o, i), s_v)
    p.wait("tensor", s_x, bias_loaded)
    p.wait("tensor", s_v, max(const_ready, t32_rel))
    p.emit("tensor", lambda e, o=p_t32[0:128, 0:16], l=ones1f, r=bias1:
           e.matmul(o, l[:], r[:], start=True, stop=True), s_mm)
    p.wait("vector", s_mm, s_mm.count)
    p.emit("vector", lambda e, o=biasb, i=p_t32[0:128, 0:16]: e.tensor_copy(o[:], i), s_v)
    for dc in range(16):
        tp.group([centnat[0:16, dc * 128:(dc + 1) * 128]],
                 [cent_fm[:, dc * 16:(dc + 1) * 16]], src_ready=(s_cw, b_loaded))

    emit_ln(p, lb, x_dram, s_x, s_v, s_sc, u_tok, tp=tp, u_fm=u_fm, u_tiles=2)
    u_fm_ready = tp.s_ev.count

    p.wait("tensor", tp.s_ev, u_fm_ready)
    sc_rel = 0
    for tt in range(NTILE):
        bank = p_a[tt % 2]
        p.wait("tensor", s_sc, pa_rel[tt % 2])
        for dc in range(16):
            lhs = u_fm[:, dc * 1024 + tt * 128: dc * 1024 + (tt + 1) * 128]
            rhs = cent_fm[:, dc * 16:(dc + 1) * 16]
            p.emit("tensor", lambda e, o=bank[:, 0:16], l=lhs, r=rhs,
                   s=(dc == 0), t2=(dc == 15): e.matmul(o, l, r, start=s, stop=t2),
                   s_mm if dc == 15 else None)
        p.wait("scalar", s_mm, s_mm.count)
        p.wait("scalar", s_v, sc_rel)
        pa_rel[tt % 2] = p.emit("scalar", lambda e, o=sc_sb[:], i=bank[:, 0:16]:
                                e.activation(out=o, in_=i, func=AF.Sigmoid), s_sc)
        p.wait("vector", s_sc, s_sc.count)
        gv = g_sb[:, tt * 16:(tt + 1) * 16]
        p.emit("vector", lambda e, o=sc_sb, i=biasb: e.tensor_tensor(
            out=o[:], in0=o[:], in1=i[:], op=ALU.add), s_v)
        p.wait("vector", s_v)
        p.emit("vector", lambda e, o=gv, i=sc_sb: e.tensor_copy(o, i[:]), s_v)
        p.wait("vector", s_v)
        for it in range(3):
            p.emit("vector", lambda e, o=m_sb, i=gv: e.reduce_max(o[:], i, axis=AX.X), s_v)
            p.wait("vector", s_v)
            p.emit("vector", lambda e, o=msk_sb, i=gv, m=m_sb: e.tensor_scalar(
                out=o[:], in0=i, scalar1=m[:], scalar2=None,
                op0=ALU.is_ge, op1=ALU.bypass), s_v)
            p.wait("vector", s_v)
            p.emit("vector", lambda e, o=gv, i=msk_sb, g2=gv: e.scalar_tensor_tensor(
                out=o, in0=i[:], scalar=-1e30, in1=g2, op0=ALU.mult, op1=ALU.add), s_v)
            p.wait("vector", s_v)
        p.emit("vector", lambda e, o=m_sb, i=gv: e.reduce_max(o[:], i, axis=AX.X), s_v)
        p.wait("vector", s_v)
        p.emit("vector", lambda e, o=msk_sb, i=sc_sb, m=m_sb: e.tensor_scalar(
            out=o[:], in0=i[:], scalar1=m[:], scalar2=None,
            op0=ALU.is_ge, op1=ALU.bypass), s_v)
        p.wait("vector", s_v)
        p.emit("vector", lambda e, o=gv, a=msk_sb, b=sc_sb: e.tensor_tensor(
            out=o, in0=a[:], in1=b[:], op=ALU.mult), s_v)
        sc_rel = s_v.count
        p.wait("vector", s_v)
        p.emit("vector", lambda e, o=sum_sb, i=gv: e.reduce_sum(o[:], i, axis=AX.X), s_v)
        p.wait("vector", s_v)
        p.emit("vector", lambda e, o=sum_sb: e.tensor_scalar_max(o[:], o[:], 1e-8), s_v)
        p.wait("vector", s_v)
        p.emit("vector", lambda e, o=sum_sb: e.reciprocal(o[:], o[:]), s_v)
        p.wait("vector", s_v)
        p.emit("vector", lambda e, o=gv, m=sum_sb: e.tensor_scalar(
            out=o, in0=o, scalar1=m[:], scalar2=None, op0=ALU.mult, op1=ALU.bypass), s_v)

    if dbg is not None and "g" in dbg:
        p.wait("scalar", s_v, s_v.count)
        for tt in range(NTILE):
            p.dma("scalar", dbg["g"][tt * 128:(tt + 1) * 128, :],
                  g_sb[:, tt * 16:(tt + 1) * 16], s_out)
    for t in range(NTILE):
        for g in range(4):
            srcs = [u_fm[:, (g * 4 + j) * 1024 + t * 128:
                         (g * 4 + j) * 1024 + (t + 1) * 128] for j in range(4)]
            dsts = [acc[:, t * 2048 + (g * 4 + j) * 128:
                        t * 2048 + (g * 4 + j + 1) * 128] for j in range(4)]
            tp.group(srcs, dsts, src_ready=(tp.s_ev, u_fm_ready))
    acc_ready = tp.s_ev.count
    if dbg is not None and "u" in dbg:
        p.wait("scalar", tp.s_ev, acc_ready)
        for tt in range(NTILE):
            p.dma("scalar", dbg["u"][tt * 128:(tt + 1) * 128, :],
                  acc[:, tt * 2048:(tt + 1) * 2048], s_out)

    w1_rel = {0: 0, 1: 0}
    w2_rel = 0
    b2_rel = {0: 0, 1: 0}
    srcs_w1 = [W["sw1"][0], W["sw1"][1]] + [W["rw1"][j] for j in range(16)]
    srcs_w2 = [W["sw2"][0], W["sw2"][1]] + [W["rw2"][j] for j in range(16)]

    w1_ctr = [0]

    def load_w1col(j, fc):
        buf = w1_ctr[0] % 2
        w1_ctr[0] += 1
        src = srcs_w1[j][:, fc * 128:(fc + 1) * 128].rearrange(
            "(dc p) f -> p dc f", p=128)
        dst = w1col[:, buf * 2048:(buf + 1) * 2048].rearrange(
            "p (dc f) -> p dc f", dc=16)
        w = w1_s.dma("gpsimd", dst, src, gate=(s_mm, w1_rel[buf]), slot=buf)
        return (w, buf)

    w1_loaded = {(0, 0): load_w1col(0, 0)}
    for j in range(18):
        src_b2 = (W["sb2"][j] if j < 2 else W["rb2"][j - 2])
        b2_loaded = b2_s.dma("gpsimd", b2buf[:], src_b2[None, :],
                             gate=(s_mm, b2_rel[0]), slot=0)
        w2_loaded = w2_s.dma("gpsimd",
                             w2_sb[:].rearrange("p (fc d) -> p fc d", fc=11),
                             srcs_w2[j].rearrange("(fc p) d -> p fc d", p=128),
                             gate=(s_mm, w2_rel), slot=0)
        for fc in range(11):
            if fc + 1 < 11:
                w1_loaded[(j, fc + 1)] = load_w1col(j, fc + 1)
            elif j + 1 < 18:
                w1_loaded[(j + 1, 0)] = load_w1col(j + 1, 0)
            w1_cnt, buf = w1_loaded[(j, fc)]
            p.wait("tensor", w1_cnt[0], w1_cnt[1])
            for th in range(2):
                bank = p_a[th]
                p.wait("tensor", s_sc, pa_rel[th])
                for dc in range(16):
                    lhs = w1col[:, buf * 2048 + dc * 128: buf * 2048 + (dc + 1) * 128]
                    rhs = u_fm[:, dc * 1024 + th * 512: dc * 1024 + (th + 1) * 512]
                    p.emit("tensor", lambda e, o=bank, l=lhs, r=rhs, s=(dc == 0),
                           t2=(dc == 15): e.matmul(o[:], l, r, start=s, stop=t2),
                           s_mm if dc == 15 else None)
                if th == 1:
                    w1_rel[buf] = s_mm.count
                p.wait("scalar", s_mm, s_mm.count)
                hv = h_sb[:, fc * 1024 + th * 512: fc * 1024 + (th + 1) * 512]
                pa_rel[th] = p.emit("scalar", lambda e, o=hv, i=bank,
                                    b=b1cols[:, fc * 18 + j: fc * 18 + j + 1]:
                                    e.activation(out=o, in_=i[:], func=AF.Relu, bias=b),
                                    s_sc)
        h_ready = s_sc.count
        p.wait("tensor", w2_loaded[0], w2_loaded[1])
        p.wait("tensor", b2_loaded[0], b2_loaded[1])
        p.wait("tensor", s_sc, h_ready)
        if j == 0:
            p.wait("tensor", s_v, const_ready)
        for tt in range(NTILE):
            for d4 in range(4):
                bi = (tt * 4 + d4) % 2
                bank = p_b[bi]
                p.wait("tensor", s_v, pb_rel[bi])
                for fc in range(11):
                    lhs = h_sb[:, fc * 1024 + tt * 128: fc * 1024 + (tt + 1) * 128]
                    rhs = w2_sb[:, fc * 2048 + d4 * 512: fc * 2048 + (d4 + 1) * 512]
                    p.emit("tensor", lambda e, o=bank, l=lhs, r=rhs, s=(fc == 0),
                           t2=(fc == 10): e.matmul(o[:], l, r, start=s, stop=t2),
                           s_mm if fc == 10 else None)
                mm_done = s_mm.count
                p.wait("vector", s_mm, mm_done)
                if j == 0 and tt == 0 and d4 == 0:
                    p.wait("vector", tp.s_ev, acc_ready)
                av = acc[:, tt * 2048 + d4 * 512: tt * 2048 + (d4 + 1) * 512]
                g = onescol[:] if j < 2 else g_sb[:, tt * 16 + (j - 2): tt * 16 + (j - 1)]
                pb_rel[bi] = p.emit(
                    "vector", lambda e, o=av, i=bank, gc=g:
                    e.scalar_tensor_tensor(out=o, in0=i[:], scalar=gc, in1=o,
                                           op0=ALU.mult, op1=ALU.add), s_v)
        h_rel = s_mm.count
        w2_rel = s_mm.count
        b2_rel[0] = s_mm.count

    emit_residual_out(p, lb, acc, x_dram, xo_dram, coef, s_x, s_v, s_out)


def emit_attn0_qkv(p, nc, x_dram, wq, wk, wv, cosd, sind, qout, kfm, vout,
                   ident_sb, tag):
    s_x = p.sem(); s_v = p.sem(); s_sc = p.sem()
    s_mm = p.sem(); s_out = p.sem()
    w_s = Stream(p, 2, swdge=True)
    tp = TP(p, ident_sb)
    lb = LNBufs(nc, tag)
    u_tok = SB(nc, f"utok{tag}", [128, 2 * 2048], BF16)
    u_fm = SB(nc, f"ufm{tag}", [128, 16 * 1024], BF16)
    w_sb = SB(nc, f"wsb{tag}", [128, 2 * 16 * 512], BF16)
    evt = SB(nc, f"evt{tag}", [128, 2 * 512], BF16)
    rot = SB(nc, f"rot{tag}", [128, 2 * 512], BF16)
    t1 = SB(nc, f"t1{tag}", [128, 256], F32)
    t2 = SB(nc, f"t2{tag}", [128, 256], F32)
    cos_sb = SB(nc, f"cos{tag}", [128, 8 * 128], F32)
    sin_sb = SB(nc, f"sin{tag}", [128, 8 * 128], F32)
    k_fm_sb = SB(nc, f"kfm{tag}", [128, 8 * 1024], BF16)
    pq = [PS(nc, f"pq{tag}{i}", [128, 512], F32) for i in range(2)]
    pq_rel = [0, 0]

    p.dma("sync", cos_sb[:].rearrange("p (t j) -> p t j", t=8),
          cosd.rearrange("(t p) j -> p t j", p=128), s_x)
    p.dma("sync", sin_sb[:].rearrange("p (t j) -> p t j", t=8),
          sind.rearrange("(t p) j -> p t j", p=128), s_x)
    tab_ready = s_x.count

    emit_ln(p, lb, x_dram, s_x, s_v, s_sc, u_tok, tp=tp, u_fm=u_fm, u_tiles=2)
    u_fm_ready = tp.s_ev.count

    jobs = [(wq, qout, 8, True, False), (wk, None, 2, True, True),
            (wv, vout, 1, False, False)]
    w_rel = {0: 0, 1: 0}
    ev_rel = {0: 0, 1: 0}
    wbuf_i = 0
    p.wait("tensor", tp.s_ev, u_fm_ready)
    for (wap, outap, n8, do_rope, do_fm) in jobs:
        for of8 in range(n8):
            wb = wbuf_i % 2
            wbuf_i += 1
            src = wap[:, of8 * 512:(of8 + 1) * 512].rearrange(
                "(dc p) f -> p dc f", p=128)
            dst = w_sb[:, wb * 8192:(wb + 1) * 8192].rearrange(
                "p (dc f) -> p dc f", dc=16)
            w_loaded = w_s.dma("gpsimd", dst, src, gate=(s_mm, w_rel[wb]), slot=wb)
            p.wait("tensor", w_loaded[0], w_loaded[1])
            for tt in range(NTILE):
                bank = pq[tt % 2]
                p.wait("tensor", s_v, pq_rel[tt % 2])
                for dc in range(NDC):
                    lhs = u_fm[:, dc * 1024 + tt * 128: dc * 1024 + (tt + 1) * 128]
                    rhs = w_sb[:, wb * 8192 + dc * 512: wb * 8192 + (dc + 1) * 512]
                    p.emit("tensor", lambda e, o=bank, l=lhs, r=rhs, s=(dc == 0),
                           t2_=(dc == 15): e.matmul(o[:], l, r, start=s, stop=t2_),
                           s_mm if dc == 15 else None)
                if tt == NTILE - 1:
                    w_rel[wb] = s_mm.count
                mm_done = s_mm.count
                eb = tt % 2
                ev = evt[:, eb * 512:(eb + 1) * 512]
                ro = rot[:, eb * 512:(eb + 1) * 512]
                p.wait("vector", s_mm, mm_done)
                p.wait("vector", s_out, ev_rel[eb])
                pq_rel[tt % 2] = p.emit("vector", lambda e, o=ev, i=bank:
                                        e.tensor_copy(o, i[:]), s_v)
                if do_rope:
                    p.wait("vector", s_x, tab_ready)
                    for hh in range(2):
                        base = hh * 256
                        re = ev[:, base:base + 256].rearrange("p (j two) -> p two j", two=2)[:, 0]
                        im = ev[:, base:base + 256].rearrange("p (j two) -> p two j", two=2)[:, 1]
                        reo = ro[:, base:base + 256].rearrange("p (j two) -> p two j", two=2)[:, 0]
                        imo = ro[:, base:base + 256].rearrange("p (j two) -> p two j", two=2)[:, 1]
                        cs = cos_sb[:, tt * 128:(tt + 1) * 128]
                        sn = sin_sb[:, tt * 128:(tt + 1) * 128]
                        p.emit("vector", lambda e, o=t1, a=re, b=cs:
                               e.tensor_tensor(out=o[:, 0:128], in0=a, in1=b, op=ALU.mult))
                        p.emit("vector", lambda e, o=t2, a=im, b=sn:
                               e.tensor_tensor(out=o[:, 0:128], in0=a, in1=b, op=ALU.mult), s_v)
                        p.wait("vector", s_v)
                        p.emit("vector", lambda e, o=reo, a=t1, b=t2:
                               e.tensor_tensor(out=o, in0=a[:, 0:128], in1=b[:, 0:128],
                                               op=ALU.subtract))
                        p.emit("vector", lambda e, o=t1, a=re, b=sn:
                               e.tensor_tensor(out=o[:, 0:128], in0=a, in1=b, op=ALU.mult))
                        p.emit("vector", lambda e, o=t2, a=im, b=cs:
                               e.tensor_tensor(out=o[:, 0:128], in0=a, in1=b, op=ALU.mult), s_v)
                        p.wait("vector", s_v)
                        p.emit("vector", lambda e, o=imo, a=t1, b=t2:
                               e.tensor_tensor(out=o, in0=a[:, 0:128], in1=b[:, 0:128],
                                               op=ALU.add), s_v)
                    outsrc = ro
                else:
                    outsrc = ev
                fin = s_v.count
                if do_fm:
                    for j4 in range(2):
                        srcs = [outsrc[:, (j4 * 2 + jj) * 128:(j4 * 2 + jj + 1) * 128]
                                for jj in range(2)]
                        dsts = [k_fm_sb[:, (of8 * 4 + j4 * 2 + jj) * 1024 + tt * 128:
                                        (of8 * 4 + j4 * 2 + jj) * 1024 + (tt + 1) * 128]
                                for jj in range(2)]
                        tp.group(srcs, dsts, src_ready=(s_v, fin))
                    ev_rel[eb] = tp.s_tp.count * 0 + s_out.count
                    ev_rel[eb] = 0
                    p.wait("vector", tp.s_tp, tp.s_tp.count)
                else:
                    p.wait("scalar", s_v, fin)
                    p.dma("scalar", outap[tt * 128:(tt + 1) * 128,
                                          of8 * 512:(of8 + 1) * 512], outsrc, s_out)
                    ev_rel[eb] = s_out.count
    kfm_ready = tp.s_ev.count
    p.wait("scalar", tp.s_ev, kfm_ready)
    for fc in range(8):
        p.dma("scalar", kfm[fc * 128:(fc + 1) * 128, :],
              k_fm_sb[:, fc * 1024:(fc + 1) * 1024], s_out)
    p.wait("scalar", s_out, s_out.count)


def emit_attn0_heads(p, nc, q_dram, kg, vg, lq1, lq2, lk1, lk2, att_out,
                     masks_d, ident_sb, ident32_sb, tag):
    s_x = p.sem(); s_v = p.sem(); s_sc = p.sem(); s_k = p.sem_swdge()
    s_mm = p.sem(); s_out = p.sem(); s_c = p.sem()
    tp = TP(p, ident_sb)
    k_sb = SB(nc, f"k{tag}", [128, 2 * 8192], BF16)
    v_sb = SB(nc, f"v{tag}", [128, 8192], BF16)
    p0_sb = SB(nc, f"p0{tag}", [128, 8192], BF16)
    p1_sb = SB(nc, f"p1{tag}", [128, 8192], BF16)
    att_sb = SB(nc, f"att{tag}", [128, 2 * 4 * 256], BF16)
    qst = SB(nc, f"qst{tag}", [128, 2 * 4 * 256], BF16)
    q_fm = SB(nc, f"qfm{tag}", [128, 2 * 2 * 4 * 128], BF16)
    msk_sb = SB(nc, f"msk{tag}", [128, 4 * 512], F32)
    wT = SB(nc, f"wT{tag}", [128, 2 * 128], BF16)
    zb = SB(nc, f"zb{tag}", [128, 2 * 8], F32)
    Z0 = SB(nc, f"Z0{tag}", [128, 1], F32)
    Z1 = SB(nc, f"Z1{tag}", [128, 1], F32)
    rz0 = SB(nc, f"rz0{tag}", [128, 1], F32)
    rz1 = SB(nc, f"rz1{tag}", [128, 1], F32)
    rz2 = SB(nc, f"rz2{tag}", [128, 1], F32)
    Z2 = SB(nc, f"Z2{tag}", [128, 1], F32)
    c1n = SB(nc, f"c1n{tag}", [128, 1], F32)
    lamv = SB(nc, f"lam{tag}", [2, 260], F32)
    lam_b = SB(nc, f"lamb{tag}", [128, 2], F32)
    lamrow = SB(nc, f"lamr{tag}", [1, 2], F32)
    ones1f = SB(nc, f"onf{tag}", [1, 128], F32)
    ssb = SB(nc, f"ss{tag}", [128, 4 * 2], F32)
    rzb = SB(nc, f"rzb{tag}", [128, 4 * 2], F32)
    hsc = SB(nc, f"hsc{tag}", [128, 2], F32)
    junk = SB(nc, f"jk{tag}", [128, 128], BF16)
    lqk = SB(nc, f"lqk{tag}", [2, 4 * 128], F32)
    bm20 = SB(nc, f"bm20{tag}", [128, 1], F32)
    bm2 = SB(nc, f"bm2{tag}", [128, 1], F32)

    pe = [PS(nc, f"pe{tag}{i}", [128, 1024], F32) for i in range(2)]
    patt = PS(nc, f"patt{tag}", [128, 512], F32)

    p.emit("vector", lambda e: e.memset(ones1f[:], 1.0), s_v)
    p.emit("vector", lambda e: e.memset(bm20[:], -20.0), s_v)
    p.emit("vector", lambda e: e.memset(bm2[:], -2.0), s_v)
    bias_ready = s_v.count
    p.dma("sync", msk_sb[:], masks_d[:], s_c)
    for i, ap in enumerate([lq1, lq2, lk1, lk2]):
        p.dma("sync", lqk[0:2, i * 128:(i + 1) * 128], ap[:], s_c)
    const_in = s_c.count
    p.wait("scalar", s_v, bias_ready)
    p.dma("gpsimd", k_sb[:, 0:8192], kg[0:128, :], s_k)
    p.dma("gpsimd", k_sb[:, 8192:16384], kg[128:256, :], s_k)
    p.dma("gpsimd", v_sb[:].rearrange("p (b f) -> p b f", f=128),
          vg[:].rearrange("(b p) f -> p b f", p=128), s_k)
    kv_loaded = s_k.count
    p.wait("vector", s_c, const_in)
    p.emit("vector", lambda e: e.tensor_tensor(
        out=lamv[0:2, 0:128], in0=lqk[0:2, 0:128], in1=lqk[0:2, 256:384],
        op=ALU.mult), s_v)
    p.emit("vector", lambda e: e.tensor_tensor(
        out=lamv[0:2, 128:256], in0=lqk[0:2, 128:256], in1=lqk[0:2, 384:512],
        op=ALU.mult), s_v)
    p.wait("vector", s_v)
    p.emit("vector", lambda e: e.reduce_sum(lamv[0:2, 256:257], lamv[0:2, 0:128],
                                            axis=AX.X), s_v)
    p.emit("vector", lambda e: e.reduce_sum(lamv[0:2, 257:258], lamv[0:2, 128:256],
                                            axis=AX.X), s_v)
    p.wait("scalar", s_v, s_v.count)
    p.emit("scalar", lambda e: e.activation(out=lamv[0:2, 256:258],
                                            in_=lamv[0:2, 256:258],
                                            func=AF.Exp), s_sc)
    p.wait("vector", s_sc, s_sc.count)
    p.emit("vector", lambda e: e.tensor_tensor(out=lamv[0:2, 258:259],
                                               in0=lamv[0:2, 256:257],
                                               in1=lamv[0:2, 257:258],
                                               op=ALU.subtract), s_v)
    p.wait("vector", s_v)
    p.emit("vector", lambda e: e.tensor_scalar(out=lamv[0:2, 258:259],
                                               in0=lamv[0:2, 258:259], scalar1=0.2,
                                               scalar2=None, op0=ALU.add,
                                               op1=ALU.bypass), s_v)
    lam_done = s_v.count
    p.wait("tensor", s_v, lam_done)
    p.emit("tensor", lambda e: e.transpose(patt[0:1, 0:2], lamv[0:2, 258:259],
                                           ident32_sb[0:2, 0:2]), s_mm)
    p.wait("vector", s_mm, s_mm.count)
    p.emit("vector", lambda e: e.tensor_copy(lamrow[:], patt[0:1, 0:2]), s_v)
    p.wait("tensor", s_v, s_v.count)
    p.emit("tensor", lambda e: e.matmul(patt[0:128, 0:2], ones1f[:], lamrow[:],
                                        start=True, stop=True), s_mm)
    p.wait("vector", s_mm, s_mm.count)
    p.emit("vector", lambda e: e.tensor_copy(lam_b[:], patt[0:128, 0:2]), s_v)
    lam_ready = s_v.count

    SC = 1.0 / np.sqrt(128.0)
    p0_rel = 0
    p1_rel = 0
    pe_rel = [0, 0]
    att_rel = 0
    wT_rel = [0, 0]
    qst_rel = 0
    out_rel = {}
    out_all_prev = 0

    for s in range(16):
        L = (s + 1) * 512
        ng = s + 1
        for hl in range(2):
            p.wait("sync", tp.s_tp, qst_rel)
            p.wait("sync", s_out, out_rel.get(hl, 0))
            for qt in range(4):
                p.dma("sync", qst[:, (hl * 4 + qt) * 256:(hl * 4 + qt + 1) * 256],
                      q_dram[s * 512 + qt * 128: s * 512 + (qt + 1) * 128,
                             hl * 256:(hl + 1) * 256], s_x)
            q_loaded = s_x.count
            for qt in range(4):
                for c in range(2):
                    tp.group([qst[:, (hl * 4 + qt) * 256 + c * 128:
                                  (hl * 4 + qt) * 256 + (c + 1) * 128]],
                             [q_fm[:, ((hl * 2 + c) * 4 + qt) * 128:
                                   ((hl * 2 + c) * 4 + qt + 1) * 128]],
                             src_ready=(s_x, q_loaded))
            qst_rel = tp.s_tp.count
            qfm_ready = tp.s_ev.count
            for qt in range(4):
                NB = s * 4 + qt + 1
                for c in range(2):
                    pz = p0_sb if c == 0 else p1_sb
                    zv = zb[:, c * 8:(c + 1) * 8]
                    for gp in range(0, ng, 2):
                        take = min(2, ng - gp)
                        bank = pe[(gp // 2) % 2]
                        p.wait("tensor", s_sc, pe_rel[(gp // 2) % 2])
                        if gp == 0:
                            p.wait("tensor", tp.s_ev, qfm_ready)
                            p.wait("tensor", s_k, kv_loaded)
                        for g2 in range(take):
                            g = gp + g2
                            lhs = q_fm[:, ((hl * 2 + c) * 4 + qt) * 128:
                                       ((hl * 2 + c) * 4 + qt + 1) * 128]
                            rhs = k_sb[:, c * 8192 + g * 512: c * 8192 + (g + 1) * 512]
                            p.emit("tensor", lambda e, o=bank[:, g2 * 512:(g2 + 1) * 512],
                                   l=lhs, r=rhs: e.matmul(o, l, r, start=True,
                                                          stop=True), s_mm)
                        mm_done = s_mm.count
                        if gp <= s < gp + take:
                            g2 = s - gp
                            p.wait("vector", s_mm, mm_done)
                            p.emit("vector", lambda e, o=bank[:, g2 * 512:(g2 + 1) * 512],
                                   m=msk_sb[:, qt * 512:(qt + 1) * 512]:
                                   e.tensor_tensor(out=o, in0=o, in1=m, op=ALU.add), s_v)
                            p.wait("scalar", s_v, s_v.count)
                        p.wait("scalar", s_mm, mm_done)
                        if c == 1:
                            p.wait("scalar", tp.s_tp, p1_rel)
                        else:
                            p.wait("scalar", s_v, p0_rel)
                        pe_rel[(gp // 2) % 2] = p.emit(
                            "scalar", lambda e, o=pz[:, gp * 512: gp * 512 + take * 512],
                            i=bank[:, 0:take * 512], z=zv[:, gp // 2: gp // 2 + 1]:
                            e.activation(out=o, in_=i, func=AF.Exp, bias=bm20[:],
                                         scale=SC, accum_out=z), s_sc)
                    p.wait("vector", s_sc, s_sc.count)
                    Zc = Z0 if c == 0 else Z1
                    p.emit("vector", lambda e, o=Zc, i=zv[:, 0:(ng + 1) // 2]:
                           e.reduce_sum(o[:], i, axis=AX.X), s_v)
                    p.wait("vector", s_v)
                    rc = rz0 if c == 0 else rz1
                    p.emit("vector", lambda e, o=rc, i=Zc: e.reciprocal(o[:], i[:]), s_v)
                    p.wait("vector", s_v)
                p.wait("vector", s_v, lam_ready)
                p.emit("vector", lambda e, o=c1n, a=Z0, b=rz1: e.tensor_tensor(
                    out=o[:], in0=a[:], in1=b[:], op=ALU.mult), s_v)
                p.wait("vector", s_v)
                p.emit("vector", lambda e, o=c1n, i=lam_b[:, hl:hl + 1]: e.tensor_tensor(
                    out=o[:], in0=o[:], in1=i, op=ALU.mult), s_v)
                p.wait("vector", s_v)
                p.emit("vector", lambda e, o=c1n: e.tensor_scalar(
                    out=o[:], in0=o[:], scalar1=-1.0, scalar2=None, op0=ALU.mult,
                    op1=ALU.bypass), s_v)
                p.wait("vector", s_v)
                p.emit("vector", lambda e, o=p1_sb[:, 0:L], a=p1_sb[:, 0:L],
                       b=p0_sb[:, 0:L], c1=c1n: e.scalar_tensor_tensor(
                           out=o, in0=a, scalar=c1[:], in1=b, op0=ALU.mult,
                           op1=ALU.add), s_v)
                p0_rel = s_v.count
                p.wait("vector", s_v)
                p.emit("vector", lambda e, o=p1_sb[:, s * 512:(s + 1) * 512],
                       m=msk_sb[:, qt * 512:(qt + 1) * 512]:
                       e.tensor_tensor(out=o, in0=o, in1=m, op=ALU.add), s_v)
                w_done = s_v.count
                p.wait("scalar", s_v, w_done)
                exp2_done = p.emit("scalar", lambda e, o=p1_sb[:, 0:L],
                                   i=p1_sb[:, 0:L], sc2=rz0, z=Z2[:]:
                                   e.activation(out=o, in_=i, func=AF.Exp, bias=bm2[:],
                                                scale=sc2[:], accum_out=z), s_sc)
                p.wait("vector", s_sc, exp2_done)
                p.emit("vector", lambda e, o=rz2, i=Z2: e.reciprocal(o[:], i[:]), s_v)
                rz2_done = s_v.count
                first_tp = True
                for kb in range(NB):
                    slot = kb % 2
                    b = tp.i % len(tp.banks)
                    tp.i += 1
                    p.wait("tensor", tp.s_ev, tp.rel[b])
                    if first_tp:
                        p.wait("tensor", s_sc, exp2_done)
                        first_tp = False
                    bank = tp.banks[b]
                    p.emit("tensor", lambda e, o=bank[:, 0:128],
                           i=p1_sb[:, kb * 128:(kb + 1) * 128], idn=ident_sb:
                           e.transpose(o, i, idn[:]), tp.s_tp)
                    p.wait("vector", tp.s_tp, tp.s_tp.count)
                    p.wait("vector", s_mm, wT_rel[slot])
                    p.emit("vector", lambda e, o=wT[:, slot * 128:(slot + 1) * 128],
                           i=bank[:, 0:128]: e.tensor_copy(o, i), tp.s_ev)
                    tp.rel[b] = tp.s_ev.count
                    ev_done = tp.s_ev.count
                    p.wait("tensor", tp.s_ev, ev_done)
                    if kb == 0:
                        p.wait("tensor", s_v, att_rel)
                    wT_rel[slot] = p.emit(
                        "tensor", lambda e, o=patt[:, 0:128],
                        l=wT[:, slot * 128:(slot + 1) * 128],
                        r=v_sb[:, kb * 128:(kb + 1) * 128], st=(kb == 0),
                        sp=(kb == NB - 1): e.matmul(o, l, r, start=st, stop=sp),
                        s_mm)
                p1_rel = tp.s_tp.count
                av_done = s_mm.count
                p.wait("scalar", s_mm, av_done)
                sq_done = p.emit("scalar", lambda e, o=junk, i=patt[:, 0:128],
                                 z=ssb[:, qt * 2 + hl: qt * 2 + hl + 1]:
                                 e.activation(out=o[:], in_=i, func=AF.Square,
                                              accum_out=z), s_sc)
                p.wait("vector", s_sc, sq_done)
                p.wait("vector", s_v, rz2_done)
                for _op in out_all_prev:
                    p.wait("vector", _op[0], _op[1])
                p.emit("vector", lambda e, o=rzb[:, qt * 2 + hl: qt * 2 + hl + 1],
                       i=rz2: e.tensor_copy(o, i[:]), s_v)
                att_rel = p.emit("vector", lambda e,
                                 o=att_sb[:, (hl * 4 + qt) * 256:
                                          (hl * 4 + qt) * 256 + 128],
                                 i=patt[:, 0:128], r=rz2:
                                 e.tensor_scalar(out=o, in0=i, scalar1=r[:],
                                                 scalar2=None, op0=ALU.mult,
                                                 op1=ALU.bypass), s_v)
        for qt in range(4):
            p.emit("vector", lambda e, o=hsc, a=ssb[:, qt * 2:(qt + 1) * 2],
                   b=rzb[:, qt * 2:(qt + 1) * 2]:
                   e.tensor_tensor(out=o[:], in0=a, in1=b, op=ALU.mult), s_v)
            p.wait("vector", s_v)
            p.emit("vector", lambda e, o=hsc, b=rzb[:, qt * 2:(qt + 1) * 2]:
                   e.tensor_tensor(out=o[:], in0=o[:], in1=b, op=ALU.mult), s_v)
            p.wait("vector", s_v)
            p.emit("vector", lambda e, o=hsc: e.tensor_scalar(
                out=o[:], in0=o[:], scalar1=1.0 / 128.0, scalar2=1e-8,
                op0=ALU.mult, op1=ALU.add), s_v)
            p.wait("scalar", s_v, s_v.count)
            p.emit("scalar", lambda e, o=hsc: e.activation(out=o[:], in_=o[:],
                                                           func=AF.Ln), s_sc)
            p.wait("scalar", s_sc, s_sc.count)
            ex_done = p.emit("scalar", lambda e, o=hsc: e.activation(
                out=o[:], in_=o[:], func=AF.Exp, scale=-0.5), s_sc)
            p.wait("vector", s_sc, ex_done)
            for hl in range(2):
                p.emit("vector", lambda e,
                       o=att_sb[:, (hl * 4 + qt) * 256:(hl * 4 + qt) * 256 + 128],
                       sc2=hsc[:, hl:hl + 1]:
                       e.tensor_scalar(out=o, in0=o, scalar1=sc2, scalar2=None,
                                       op0=ALU.mult, op1=ALU.bypass), s_v)
        att_scaled = s_v.count
        p.wait("scalar", s_v, att_scaled)
        for hl in range(2):
            for qt in range(4):
                p.dma("scalar",
                      att_out[s * 512 + qt * 128: s * 512 + (qt + 1) * 128,
                              hl * 128:(hl + 1) * 128],
                      att_sb[:, (hl * 4 + qt) * 256:(hl * 4 + qt) * 256 + 128],
                      s_out)
            out_rel[hl] = s_out.count
        out_all_prev = s_out.count
    p.wait("scalar", s_out, s_out.count)


def emit_attn0_out(p, nc, x_dram, xo_dram, att_dram, wo, rms_w, ident_sb, tag):
    s_x = p.sem(); s_v = p.sem(); s_sc = p.sem()
    s_mm = p.sem(); s_out = p.sem(); s_r = p.sem()
    at_s = Stream(p, 2)
    wo_s = Stream(p, 2, swdge=True)
    tp = TP(p, ident_sb)
    att_tok = SB(nc, f"at{tag}", [128, 2 * 2048], BF16)
    att_fm = SB(nc, f"afm{tag}", [128, 16 * 1024], BF16)
    wo_sb = SB(nc, f"wo{tag}", [128, 2 * 16 * 512], BF16)
    x_sb = SB(nc, f"xs{tag}", [128, 8 * 2048], F32)
    rmsw = SB(nc, f"rw{tag}", [128, 1], F32)
    po = [PS(nc, f"po{tag}{i}", [128, 512], F32) for i in range(2)]
    po_rel = [0, 0]

    p.dma("sync", rmsw[:], rms_w[:, None], s_r)
    p.wait("vector", s_r, s_r.count)
    p.emit("vector", lambda e: e.tensor_scalar(out=rmsw[:], in0=rmsw[:], scalar1=0.8,
                                               scalar2=None, op0=ALU.mult,
                                               op1=ALU.bypass), s_v)
    rmsw_ready = s_v.count

    at_rel = {0: 0, 1: 0}
    for t in range(NTILE):
        ab = t % 2
        av = att_tok[:, ab * 2048:(ab + 1) * 2048]
        a_loaded = at_s.dma("sync", av, att_dram[t * 128:(t + 1) * 128, :],
                            gate=(tp.s_tp, at_rel[ab]), slot=ab)
        for g in range(4):
            srcs = [av[:, (g * 4 + j) * 128:(g * 4 + j + 1) * 128] for j in range(4)]
            dsts = [att_fm[:, (g * 4 + j) * 1024 + t * 128:
                           (g * 4 + j) * 1024 + (t + 1) * 128] for j in range(4)]
            tp.group(srcs, dsts, src_ready=a_loaded)
        at_rel[ab] = tp.s_tp.count
    attfm_ready = tp.s_ev.count

    for t in range(NTILE):
        p.dma("sync", x_sb[:, t * 2048:(t + 1) * 2048],
              x_dram[t * 128:(t + 1) * 128, :], s_x)
    x_all = s_x.count

    wo_rel = {0: 0, 1: 0}
    for oc in range(4):
        wb = oc % 2
        wo_loaded = wo_s.dma(
            "gpsimd", wo_sb[:, wb * 8192:(wb + 1) * 8192].rearrange(
                "p (ic f) -> p ic f", ic=16),
            wo[:, oc * 512:(oc + 1) * 512].rearrange("(ic p) f -> p ic f", p=128),
            gate=(s_mm, wo_rel[wb]), slot=wb)
        p.wait("vector", wo_loaded[0], wo_loaded[1])
        p.wait("vector", s_v, rmsw_ready)
        wo_scaled = p.emit("vector", lambda e, o=wo_sb[:, wb * 8192:(wb + 1) * 8192],
                           r=rmsw: e.tensor_scalar(out=o, in0=o, scalar1=r[:],
                                                   scalar2=None, op0=ALU.mult,
                                                   op1=ALU.bypass), s_v)
        for t in range(NTILE):
            bank = po[t % 2]
            p.wait("tensor", s_v, po_rel[t % 2])
            p.wait("tensor", s_v, wo_scaled)
            if oc == 0 and t == 0:
                p.wait("tensor", tp.s_ev, attfm_ready)
            for ic in range(16):
                lhs = att_fm[:, ic * 1024 + t * 128: ic * 1024 + (t + 1) * 128]
                rhs = wo_sb[:, wb * 8192 + ic * 512: wb * 8192 + (ic + 1) * 512]
                p.emit("tensor", lambda e, o=bank, l=lhs, r=rhs,
                       st=(ic == 0), sp=(ic == 15):
                       e.matmul(o[:], l, r, start=st, stop=sp),
                       s_mm if ic == 15 else None)
            if t == NTILE - 1:
                wo_rel[wb] = s_mm.count
            prj_done = s_mm.count
            xv = x_sb[:, t * 2048 + oc * 512: t * 2048 + (oc + 1) * 512]
            p.wait("vector", s_x, x_all)
            p.wait("vector", s_mm, prj_done)
            po_rel[t % 2] = p.emit("vector", lambda e, o=xv, i=bank:
                                   e.tensor_tensor(out=o, in0=o, in1=i[:],
                                                   op=ALU.add), s_v)
            if oc == 3:
                p.wait("scalar", s_v, s_v.count)
                p.dma("scalar", xo_dram[t * 128:(t + 1) * 128, :],
                      x_sb[:, t * 2048:(t + 1) * 2048], s_out)
    p.wait("scalar", s_out, s_out.count)


def emit_attn1_qkv(p, nc, x_dram, wqkv, wg1, wg2, ld_dram, qo, ko, vo, go, mout,
                   colv4_d, ident_sb, tag):
    s_x = p.sem(); s_v = p.sem(); s_sc = p.sem()
    s_mm = p.sem(); s_out = p.sem(); s_gw = p.sem_swdge()
    w_s = Stream(p, 2, swdge=True)
    tp = TP(p, ident_sb)
    lb = LNBufs(nc, tag)
    u_tok = SB(nc, f"utok{tag}", [128, 2 * 2048], BF16)
    u_fm = SB(nc, f"ufm{tag}", [128, 16 * 1024], BF16)
    w_sb = SB(nc, f"wsb{tag}", [128, 2 * 16 * 512], BF16)
    k_sb = SB(nc, f"ksb{tag}", [128, 8 * 2048], BF16)
    v_sb = SB(nc, f"vsb{tag}", [128, 8 * 2048], BF16)
    ev_sb = SB(nc, f"ev{tag}", [128, 2 * 512], BF16)
    g1fm = SB(nc, f"g1f{tag}", [128, 8 * 128], BF16)
    wg1_sb = SB(nc, f"wg1{tag}", [128, 16 * 128], BF16)
    wg2_sb = SB(nc, f"wg2{tag}", [128, 2048], BF16)
    colv4 = SB(nc, f"cv4{tag}", [1, 4 * 128], F32)
    ldrow = SB(nc, f"ldr{tag}", [1, 16], F32)
    kdecS = SB(nc, f"kdS{tag}", [128, 4 * 16], F32)
    kdk = SB(nc, f"kdk{tag}", [128, 128], BF16)
    mst = SB(nc, f"mst{tag}", [128, 128], F32)
    pq = [PS(nc, f"pq{tag}{i}", [128, 512], F32) for i in range(2)]
    pm = PS(nc, f"pm{tag}", [128, 128], F32)
    pq_rel = [0, 0]

    p.dma("sync", colv4[:], colv4_d[:], s_x)
    p.dma("sync", ldrow[:], ld_dram[None, :], s_x)
    cready = s_x.count
    p.wait("tensor", s_x, cready)
    for c in range(4):
        p.emit("tensor", lambda e, o=pm[0:128, 0:16],
               l=colv4[:, c * 128:(c + 1) * 128], r=ldrow:
               e.matmul(o, l, r[:], start=True, stop=True), s_mm)
        p.wait("scalar", s_mm, s_mm.count)
        p.emit("scalar", lambda e, o=kdecS[:, c * 16:(c + 1) * 16], i=pm[0:128, 0:16]:
               e.activation(out=o, in_=i, func=AF.Exp), s_sc)
        p.wait("tensor", s_sc, s_sc.count)

    emit_ln(p, lb, x_dram, s_x, s_v, s_sc, u_tok, tp=tp, u_fm=u_fm, u_tiles=2)
    u_fm_ready = tp.s_ev.count
    p.wait("tensor", tp.s_ev, u_fm_ready)

    w_rel = {0: 0, 1: 0}
    ev_rel = {0: None, 1: None}
    es_ = Stream(p, 2)
    wbuf_i = 0
    for wi, (wap, outap) in enumerate(zip(wqkv, [qo, ko, vo])):
        for of4 in range(4):
            wb = wbuf_i % 2
            wbuf_i += 1
            src = wap[:, of4 * 512:(of4 + 1) * 512].rearrange(
                "(dc p) f -> p dc f", p=128)
            dst = w_sb[:, wb * 8192:(wb + 1) * 8192].rearrange(
                "p (dc f) -> p dc f", dc=16)
            w_loaded = w_s.dma("gpsimd", dst, src, gate=(s_mm, w_rel[wb]), slot=wb)
            p.wait("tensor", w_loaded[0], w_loaded[1])
            for tt in range(NTILE):
                bank = pq[tt % 2]
                p.wait("tensor", s_sc, pq_rel[tt % 2])
                for dc in range(NDC):
                    lhs = u_fm[:, dc * 1024 + tt * 128: dc * 1024 + (tt + 1) * 128]
                    rhs = w_sb[:, wb * 8192 + dc * 512: wb * 8192 + (dc + 1) * 512]
                    p.emit("tensor", lambda e, o=bank, l=lhs, r=rhs, s=(dc == 0),
                           t2_=(dc == 15): e.matmul(o[:], l, r, start=s, stop=t2_),
                           s_mm if dc == 15 else None)
                if tt == NTILE - 1:
                    w_rel[wb] = s_mm.count
                mm_done = s_mm.count
                p.wait("scalar", s_mm, mm_done)
                if wi == 1:
                    dstv = k_sb[:, tt * 2048 + of4 * 512: tt * 2048 + (of4 + 1) * 512]
                elif wi == 2:
                    dstv = v_sb[:, tt * 2048 + of4 * 512: tt * 2048 + (of4 + 1) * 512]
                else:
                    dstv = ev_sb[:, (tt % 2) * 512:(tt % 2 + 1) * 512]
                    if ev_rel[tt % 2] is not None:
                        p.wait("scalar", ev_rel[tt % 2][0], ev_rel[tt % 2][1])
                pq_rel[tt % 2] = p.emit("scalar", lambda e, o=dstv, i=bank:
                                        e.activation(out=o, in_=i[:], func=AF.Silu),
                                        s_sc)
                oslc = outap[tt * 128:(tt + 1) * 128, of4 * 512:(of4 + 1) * 512]
                if wi == 0:
                    ev_rel[tt % 2] = es_.dma("scalar", oslc, dstv, slot=tt % 2,
                                             gate=(s_sc, s_sc.count))
                else:
                    p.wait("scalar", s_sc, s_sc.count)
                    p.dma("scalar", oslc, dstv, s_out)
    qkv_done = s_sc.count

    p.dma("gpsimd", wg1_sb[:].rearrange("p (dc f) -> p dc f", dc=16),
          wg1.rearrange("(dc p) f -> p dc f", p=128), s_gw)
    p.dma("gpsimd", wg2_sb[:], wg2[:], s_gw)
    g_loaded = s_gw.count
    p.wait("tensor", s_gw, g_loaded)
    for tt in range(NTILE):
        bank = pq[tt % 2]
        p.wait("tensor", s_sc, pq_rel[tt % 2])
        p.wait("tensor", s_v, 0)
        for dc in range(NDC):
            lhs = u_fm[:, dc * 1024 + tt * 128: dc * 1024 + (tt + 1) * 128]
            rhs = wg1_sb[:, dc * 128:(dc + 1) * 128]
            p.emit("tensor", lambda e, o=bank[:, 0:128], l=lhs, r=rhs, s=(dc == 0),
                   t2_=(dc == 15): e.matmul(o, l, r, start=s, stop=t2_),
                   s_mm if dc == 15 else None)
        g1_done = s_mm.count
        p.wait("scalar", s_mm, g1_done)
        eb = tt % 2
        if ev_rel[eb] is not None:
            p.wait("scalar", ev_rel[eb][0], ev_rel[eb][1])
        ev = ev_sb[:, eb * 512: eb * 512 + 128]
        pq_rel[tt % 2] = p.emit("scalar", lambda e, o=ev, i=bank[:, 0:128]:
                                e.activation(out=o, in_=i, func=AF.Copy), s_sc)
        g1ev = s_sc.count
        tp.group([ev], [g1fm[:, tt * 128:(tt + 1) * 128]], src_ready=(s_sc, g1ev))
        ev_rel[eb] = None
    g1fm_ready = tp.s_ev.count
    p.wait("tensor", tp.s_ev, g1fm_ready)
    p.wait("scalar", tp.s_tp, tp.s_tp.count)
    for tt in range(NTILE):
        for oc in range(4):
            bank = pq[(tt * 4 + oc) % 2]
            p.wait("tensor", s_sc, pq_rel[(tt * 4 + oc) % 2])
            p.emit("tensor", lambda e, o=bank,
                   l=g1fm[:, tt * 128:(tt + 1) * 128],
                   r=wg2_sb[:, oc * 512:(oc + 1) * 512]:
                   e.matmul(o[:], l, r, start=True, stop=True), s_mm)
            mm_done = s_mm.count
            p.wait("scalar", s_mm, mm_done)
            eb = (tt * 4 + oc) % 2
            if ev_rel[eb] is not None:
                p.wait("scalar", ev_rel[eb][0], ev_rel[eb][1])
            ev = ev_sb[:, eb * 512:(eb + 1) * 512]
            pq_rel[eb] = p.emit("scalar", lambda e, o=ev, i=bank:
                                e.activation(out=o, in_=i[:], func=AF.Copy), s_sc)
            ev_rel[eb] = es_.dma("scalar", go[tt * 128:(tt + 1) * 128,
                                              oc * 512:(oc + 1) * 512],
                                 ev, slot=eb, gate=(s_sc, s_sc.count))

    kdk_rel = 0
    m_rel = 0
    for sl in range(2):
        for h in range(16):
            for c in range(4):
                t = sl * 4 + c
                kv = k_sb[:, t * 2048 + h * 128: t * 2048 + (h + 1) * 128]
                vv = v_sb[:, t * 2048 + h * 128: t * 2048 + (h + 1) * 128]
                p.wait("vector", s_sc, qkv_done)
                p.wait("vector", s_mm, kdk_rel)
                kd_done = p.emit("vector", lambda e, o=kdk, a=kv,
                                 s2=kdecS[:, c * 16 + h: c * 16 + h + 1]:
                                 e.tensor_scalar(out=o[:], in0=a, scalar1=s2,
                                                 scalar2=None, op0=ALU.mult,
                                                 op1=ALU.bypass), s_v)
                p.wait("tensor", s_v, kd_done)
                if c == 0:
                    p.wait("tensor", s_v, m_rel)
                kdk_rel = p.emit("tensor", lambda e, o=pm[:, 0:128], l=kdk[:], r=vv,
                                 st=(c == 0), sp=(c == 3):
                                 e.matmul(o, l, r, start=st, stop=sp), s_mm)
            p.wait("vector", s_mm, s_mm.count)
            m_rel = p.emit("vector", lambda e, o=mst, i=pm[:, 0:128]:
                           e.tensor_copy(o[:], i), s_v)
            p.wait("scalar", s_v, m_rel)
            p.dma("scalar", mout[sl, h], mst[:], s_out)
    p.wait("scalar", s_out, s_out.count)
    for s_ in es_.sems:
        if s_.count:
            p.wait("scalar", s_, s_.count)


def _finish_passthrough(p, nc, x_dram, xo_dram, xbuf, s_x, s_v, s_out, dep,
                        extra_mm=None):
    buf = SB(nc, f"ptx{p._semi}", [128, 2048], F32)
    p.wait("sync", dep[0], dep[1])
    if extra_mm is not None:
        p.wait("sync", extra_mm[0], extra_mm[1])
    prev = 0
    for t in range(NTILE):
        p.wait("sync", s_out, prev)
        p.dma("sync", buf[:], x_dram[t * 128:(t + 1) * 128, :], s_x)
        p.wait("scalar", s_x, s_x.count)
        p.dma("scalar", xo_dram[t * 128:(t + 1) * 128, :], buf[:], s_out)
        prev = s_out.count
    p.wait("scalar", s_out, s_out.count)


def emit_attn1_core(p, nc, x_dram, xo_dram, qd, kd_, vd, gd, m_all, ld_dram,
                    wo, norm_w, ecoef_d, dmat_d, ip1_d, ident_sb, ident32_sb, tag,
                    level=5, dbg=None, mh_loads=None):
    s_x = p.sem(); s_v = p.sem(); s_sc = p.sem(); s_w = p.sem()
    s_mm = p.sem(); s_out = p.sem(); s_m = p.sem(); s_g = p.sem()
    tp = TP(p, ident_sb)
    q_sb = SB(nc, f"q{tag}", [128, 8 * 2048], BF16)
    kv_sb = SB(nc, f"kv{tag}", [128, 16 * 2048], BF16)
    k_sb = kv_sb[:, 0:8 * 2048]
    v_sb = kv_sb[:, 8 * 2048:16 * 2048]
    o_sb = SB(nc, f"o{tag}", [128, 8 * 2048], BF16)
    _kv_addr = nc.lookup_mls(kv_sb).memorylocations[0].addr
    x_sb = nc.alloc_sbuf_tensor_at(f"xs{tag}", [128, 8 * 2048], F32,
                                   offset=_kv_addr)
    g_sb = SB(nc, f"g{tag}", [128, 2048], BF16)
    wo_sb = SB(nc, f"wo{tag}", [128, 2 * 16 * 512], BF16)
    Mh = SB(nc, f"Mh{tag}", [128, 16 * 128], F32)
    S_f = SB(nc, f"S{tag}", [128, 128], F32)
    S_bf = SB(nc, f"Sb{tag}", [128, 128], BF16)
    D_sb = SB(nc, f"D{tag}", [128, 16 * 128], F32)
    dmat = SB(nc, f"dm{tag}", [128, 128], F32)
    sc_T = SB(nc, f"scT{tag}", [128, 128], BF16)
    kdk = SB(nc, f"kdk{tag}", [128, 128], BF16)
    kqfm = SB(nc, f"kqf{tag}", [128, 3 * 128], BF16)
    ldrow = SB(nc, f"ldr{tag}", [1, 16], F32)
    ldb = SB(nc, f"ldb{tag}", [128, 16], F32)
    lamb = SB(nc, f"lmb{tag}", [128, 16], F32)
    qdec = SB(nc, f"qd{tag}", [128, 16], F32)
    kdec = SB(nc, f"kd{tag}", [128, 16], F32)
    ip1 = SB(nc, f"ip1{tag}", [1, 2 * 128], F32)
    ecoef = SB(nc, f"ec{tag}", [1, 32], F32)
    coefT = SB(nc, f"cT{tag}", [16, 16], F32)
    coefrow = SB(nc, f"cr{tag}", [1, 256], F32)
    coefb = SB(nc, f"cb{tag}", [128, 2 * 256], F32)
    ones1f = SB(nc, f"onf{tag}", [1, 128], F32)
    normw = SB(nc, f"nw{tag}", [16, 128], F32)
    normc = SB(nc, f"nc{tag}", [128, 16], F32)
    ssb = SB(nc, f"ss{tag}", [128, 16], F32)
    junk = SB(nc, f"jk{tag}", [128, 128], BF16)
    coef_dram = nc.dram_tensor(f"cscr{tag}", [2, 16, 16], F32, kind="Internal").ap()

    pq = [PS(nc, f"pq{tag}{i}", [128, 512], F32) for i in range(2)]
    pm = PS(nc, f"pm{tag}", [128, 512], F32)
    pm_o = PS(nc, f"pmo{tag}", [128, 128], F32)
    pm_a = PS(nc, f"pma{tag}", [128, 128], F32)
    pq_rel = [0, 0]

    for t in range(NTILE):
        p.dma("sync", q_sb[:, t * 2048:(t + 1) * 2048],
              qd[t * 128:(t + 1) * 128, :], s_x)
        p.dma("sync", k_sb[:, t * 2048:(t + 1) * 2048],
              kd_[t * 128:(t + 1) * 128, :], s_x)
        p.dma("sync", v_sb[:, t * 2048:(t + 1) * 2048],
              vd[t * 128:(t + 1) * 128, :], s_x)
    qkv_loaded = s_x.count
    p.dma("sync", dmat[:], dmat_d[:], s_m)
    p.dma("sync", ip1[:], ip1_d[:], s_m)
    p.dma("sync", ldrow[:], ld_dram[None, :], s_m)
    p.dma("sync", ecoef[:], ecoef_d.rearrange("a b -> (a b)")[None, :], s_m)
    p.dma("sync", normw[:], norm_w.rearrange("(ic p) -> ic p", p=128), s_m)
    const_in = s_m.count
    p.emit("vector", lambda e: e.memset(ones1f[:], 1.0), s_v)
    ones_ready = s_v.count
    if level == 10:
        _finish_passthrough(p, nc, x_dram, xo_dram, None, s_x, s_v, s_out,
                            (s_m, const_in), extra_mm=(s_x, qkv_loaded))
        return

    p.wait("tensor", s_m, const_in)
    p.wait("tensor", s_v, ones_ready)
    p.emit("tensor", lambda e: e.matmul(pm[0:128, 0:16], ones1f[:], ldrow[:],
                                        start=True, stop=True), s_mm)
    p.wait("vector", s_mm, s_mm.count)
    p.emit("vector", lambda e: e.tensor_copy(ldb[:], pm[0:128, 0:16]), s_v)
    p.wait("tensor", s_v, s_v.count)
    p.emit("tensor", lambda e: e.matmul(pm[0:128, 16:32], ip1[:, 0:128], ldrow[:],
                                        start=True, stop=True), s_mm)
    p.wait("scalar", s_mm, s_mm.count)
    p.emit("scalar", lambda e: e.activation(out=qdec[:], in_=pm[0:128, 16:32],
                                            func=AF.Exp), s_sc)
    p.wait("vector", s_sc, s_sc.count)
    p.emit("vector", lambda e: e.tensor_scalar(out=lamb[:], in0=ldb[:], scalar1=128.0,
                                               scalar2=None, op0=ALU.mult,
                                               op1=ALU.bypass), s_v)
    p.wait("scalar", s_v, s_v.count)
    p.emit("scalar", lambda e: e.activation(out=lamb[:], in_=lamb[:], func=AF.Exp),
           s_sc)
    p.wait("tensor", s_sc, s_sc.count)
    p.emit("tensor", lambda e: e.matmul(pm[0:128, 32:48], ip1[:, 128:256], ldrow[:],
                                        start=True, stop=True), s_mm)
    p.wait("scalar", s_mm, s_mm.count)
    p.emit("scalar", lambda e: e.activation(out=kdec[:], in_=pm[0:128, 32:48],
                                            func=AF.Exp), s_sc)
    tabs_ready = s_sc.count
    p.dbg_handles = {"ldb": ldb, "qdec": qdec, "kdec": kdec, "lamb": lamb,
                     "D_sb": D_sb, "coefb": coefb, "S_f": S_f, "dmat": dmat,
                     "ldrow": ldrow, "kqfm": kqfm, "sc_T": sc_T}
    if level == 11:
        _finish_passthrough(p, nc, x_dram, xo_dram, None, s_x, s_v, s_out,
                            (s_sc, tabs_ready))
        return
    p.wait("vector", s_sc, tabs_ready)
    p.wait("vector", s_m, const_in)
    for h in range(16):
        p.emit("vector", lambda e, o=D_sb[:, h * 128:(h + 1) * 128],
               s2=ldb[:, h:h + 1]: e.tensor_scalar(out=o, in0=dmat[:], scalar1=s2,
                                                   scalar2=None, op0=ALU.mult,
                                                   op1=ALU.bypass), s_v)
    p.wait("scalar", s_v, s_v.count)
    p.emit("scalar", lambda e: e.activation(out=D_sb[:], in_=D_sb[:], func=AF.Exp),
           s_sc)
    D_ready = s_sc.count
    if level == 12:
        _finish_passthrough(p, nc, x_dram, xo_dram, None, s_x, s_v, s_out,
                            (s_sc, D_ready))
        return
    for sl in range(2):
        p.wait("tensor", s_sc, s_sc.count)
        p.wait("tensor", s_v, s_v.count)
        p.emit("tensor", lambda e, r=ecoef[:, sl * 16:(sl + 1) * 16]:
               e.matmul(pm[0:16, 48:64], ldrow[:], r, start=True, stop=True), s_mm)
        p.wait("scalar", s_mm, s_mm.count)
        p.emit("scalar", lambda e: e.activation(out=coefT[:], in_=pm[0:16, 48:64],
                                                func=AF.Exp), s_sc)
        p.wait("scalar", s_sc, s_sc.count)
        p.dma("scalar", coef_dram[sl], coefT[:], s_m)
        p.wait("sync", s_m, s_m.count)
        p.dma("sync", coefrow[:], coef_dram[sl].rearrange("a b -> (a b)")[None, :],
              s_m)
        p.wait("tensor", s_m, s_m.count)
        p.wait("tensor", s_v, s_v.count)
        p.emit("tensor", lambda e: e.matmul(pm[0:128, 64:320], ones1f[:],
                                            coefrow[:], start=True, stop=True), s_mm)
        p.wait("vector", s_mm, s_mm.count)
        p.emit("vector", lambda e, o=coefb[:, sl * 256:(sl + 1) * 256]:
               e.tensor_copy(o, pm[0:128, 64:320]), s_v)
    coefb_ready = s_v.count
    if level == 13:
        _finish_passthrough(p, nc, x_dram, xo_dram, None, s_x, s_v, s_out,
                            (s_v, coefb_ready))
        return

    p.wait("tensor", s_m, const_in)
    p.wait("tensor", s_v, coefb_ready)
    p.emit("tensor", lambda e: e.transpose(pm[0:128, 384:400], normw[0:16, 0:128],
                                           ident32_sb[0:16, 0:16]), s_mm)
    p.wait("vector", s_mm, s_mm.count)
    p.emit("vector", lambda e: e.tensor_copy(normc[:], pm[0:128, 384:400]), s_v)
    normc_ready = s_v.count

    if level == 0:
        _finish_passthrough(p, nc, x_dram, xo_dram, x_sb, s_x, s_v, s_out,
                            (s_v, normc_ready))
        return
    kq_rel = 0
    o_rel = 0
    S_rel = normc_ready
    if level in (20, 21, 22):
        p.emit("vector", lambda e: e.memset(o_sb[:], 0.0), s_v)
        p.wait("vector", s_v)
    _hs = [0] if level in (20, 21, 22) else range(16)
    _sls = (lambda: [0]) if level in (20, 21) else (lambda: range(2))
    _cs = (lambda: [0]) if level == 20 else (lambda: range(4))
    for h in _hs:
        p.wait("sync", s_v, S_rel)
        if mh_loads is None:
            _ml = [(Mh[:].rearrange("p (s e) -> p s e", s=16),
                    m_all[:, h].rearrange("s p e -> p s e"))]
        else:
            _ml = mh_loads(Mh, h)
        for (dap, sap) in _ml:
            p.dma("sync", dap, sap, s_m)
        mh_loaded = s_m.count
        for sl in _sls():
            p.wait("vector", s_m, mh_loaded)
            p.wait("vector", s_v, coefb_ready)
            p.emit("vector", lambda e: e.memset(S_f[:], 0.0), s_v)
            p.wait("vector", s_v)
            for sg in range(16):
                p.emit("vector", lambda e, m=Mh[:, sg * 128:(sg + 1) * 128],
                       c=coefb[:, sl * 256 + h * 16 + sg: sl * 256 + h * 16 + sg + 1]:
                       e.scalar_tensor_tensor(out=S_f[:], in0=m, scalar=c, in1=S_f[:],
                                              op0=ALU.mult, op1=ALU.add), s_v)
                p.wait("vector", s_v)
            S_rel = s_v.count
            p.emit("vector", lambda e: e.tensor_copy(S_bf[:], S_f[:]), s_v)
            p.wait("vector", s_v)
            for c in _cs():
                t = sl * 4 + c
                kv = k_sb[:, t * 2048 + h * 128: t * 2048 + (h + 1) * 128]
                qv = q_sb[:, t * 2048 + h * 128: t * 2048 + (h + 1) * 128]
                vv = v_sb[:, t * 2048 + h * 128: t * 2048 + (h + 1) * 128]
                p.wait("vector", s_x, qkv_loaded)
                p.wait("vector", s_mm, kq_rel)
                kdq_done = p.emit("vector", lambda e, o=kdk, a=qv,
                                  s2=qdec[:, h:h + 1]:
                                  e.tensor_scalar(out=o[:], in0=a, scalar1=s2,
                                                  scalar2=None, op0=ALU.mult,
                                                  op1=ALU.bypass), s_v)
                tp.group([kv, qv, kdk[:]],
                         [kqfm[:, 0:128], kqfm[:, 128:256], kqfm[:, 256:384]],
                         src_ready=(s_v, kdq_done))
                kqfm_ready = tp.s_ev.count
                p.wait("tensor", tp.s_ev, kqfm_ready)
                p.wait("tensor", s_v, S_rel)
                p.emit("tensor", lambda e: e.matmul(pm[:, 0:128], kqfm[:, 0:128],
                                                    kqfm[:, 128:256], start=True,
                                                    stop=True), s_mm)
                scr_done = s_mm.count
                p.wait("vector", s_mm, scr_done)
                p.wait("vector", s_sc, D_ready)
                scT_done = p.emit("vector", lambda e, d=D_sb[:, h * 128:(h + 1) * 128]:
                                  e.tensor_tensor(out=sc_T[:], in0=pm[:, 0:128],
                                                  in1=d, op=ALU.mult), s_v)
                kdk_done = p.emit("vector", lambda e, o=kdk, a=kv,
                                  s2=kdec[:, h:h + 1]:
                                  e.tensor_scalar(out=o[:], in0=a, scalar1=s2,
                                                  scalar2=None, op0=ALU.mult,
                                                  op1=ALU.bypass), s_v)
                p.wait("tensor", s_v, scT_done)
                p.wait("tensor", s_v, o_rel)
                p.emit("tensor", lambda e, r=vv: e.matmul(pm_o[:], sc_T[:], r,
                                                          start=True, stop=False))
                p.emit("tensor", lambda e: e.matmul(pm_o[:], kqfm[:, 256:384],
                                                    S_bf[:], start=False, stop=True),
                       s_mm)
                o_done = s_mm.count
                p.wait("tensor", s_v, kdk_done)
                p.emit("tensor", lambda e, r=vv: e.matmul(pm_a[:], kdk[:], r,
                                                          start=True, stop=True), s_mm)
                kq_rel = s_mm.count
                a_done = s_mm.count
                p.wait("vector", s_mm, a_done)
                o_rel = p.emit("vector", lambda e,
                               o=o_sb[:, t * 2048 + h * 128: t * 2048 + (h + 1) * 128]:
                               e.tensor_copy(o, pm_o[:]), s_v)
                if dbg is not None and "oc" in dbg and level == 21:
                    p.wait("scalar", s_v, o_rel)
                    p.dma("scalar", dbg["oc"][c], 
                          o_sb[:, t * 2048 + h * 128: t * 2048 + (h + 1) * 128],
                          s_out)
                    p.wait("vector", s_out, s_out.count)
                p.emit("vector", lambda e, lm=lamb[:, h:h + 1]:
                       e.scalar_tensor_tensor(out=S_f[:], in0=S_f[:], scalar=lm,
                                              in1=pm_a[:], op0=ALU.mult,
                                              op1=ALU.add), s_v)
                p.wait("vector", s_v)
                p.emit("vector", lambda e: e.tensor_copy(S_bf[:], S_f[:]), s_v)
                S_rel = s_v.count
                p.wait("vector", s_v)
    o_ready = s_v.count
    main_done_mm = s_mm.count
    main_done_v = s_v.count
    if dbg is not None and "o" in dbg:
        p.wait("scalar", s_v, o_ready)
        for t in range(1 if level == 20 else 4 if level == 21 else NTILE):
            p.dma("scalar", dbg["o"][t * 128:(t + 1) * 128, :],
                  o_sb[:, t * 2048:(t + 1) * 2048], s_out)
        p.wait("vector", s_out, s_out.count)
    if level == 1:
        _finish_passthrough(p, nc, x_dram, xo_dram, None, s_x, s_v, s_out,
                            (s_v, o_ready), extra_mm=(s_mm, main_done_mm))
        return

    g_rel = 0
    for t in range(NTILE):
        p.wait("sync", s_v, g_rel)
        p.dma("sync", g_sb[:], gd[t * 128:(t + 1) * 128, :], s_g)
        g_loaded = s_g.count
        p.wait("scalar", s_g, g_loaded)
        p.wait("scalar", s_v, o_ready)
        sig_done = p.emit("scalar", lambda e: e.activation(out=g_sb[:], in_=g_sb[:],
                                                           func=AF.Sigmoid), s_sc)
        p.wait("vector", s_sc, sig_done)
        ov = o_sb[:, t * 2048:(t + 1) * 2048]
        p.emit("vector", lambda e, o=ov: e.tensor_tensor(out=o, in0=o, in1=g_sb[:],
                                                         op=ALU.mult), s_v)
        gated_done = s_v.count
        g_rel = gated_done
        p.wait("scalar", s_v, gated_done)
        for h in range(16):
            p.emit("scalar", lambda e, i=ov[:, h * 128:(h + 1) * 128],
                   z=ssb[:, h:h + 1]: e.activation(out=junk[:], in_=i,
                                                   func=AF.Square, accum_out=z),
                   s_sc)
            p.wait("scalar", s_sc, s_sc.count)
        sq_done = s_sc.count
        p.wait("vector", s_sc, sq_done)
        p.emit("vector", lambda e: e.tensor_scalar(out=ssb[:], in0=ssb[:],
                                                   scalar1=1.0 / 128.0, scalar2=1e-5,
                                                   op0=ALU.mult, op1=ALU.add), s_v)
        p.wait("scalar", s_v, s_v.count)
        p.emit("scalar", lambda e: e.activation(out=ssb[:], in_=ssb[:], func=AF.Ln),
               s_sc)
        p.wait("scalar", s_sc, s_sc.count)
        rs_done = p.emit("scalar", lambda e: e.activation(out=ssb[:], in_=ssb[:],
                                                          func=AF.Exp, scale=-0.5),
                         s_sc)
        p.wait("vector", s_sc, rs_done)
        for h in range(16):
            p.emit("vector", lambda e, o=ov[:, h * 128:(h + 1) * 128],
                   s2=ssb[:, h:h + 1]: e.tensor_scalar(out=o, in0=o, scalar1=s2,
                                                       scalar2=None, op0=ALU.mult,
                                                       op1=ALU.bypass),
                   s_v if h == 15 else None)
    o_normed = s_v.count

    if level == 2:
        _finish_passthrough(p, nc, x_dram, xo_dram, None, s_x, s_v, s_out,
                            (s_v, o_normed))
        return
    o_fm = q_sb
    for t in range(NTILE):
        for g in range(4):
            srcs = [o_sb[:, t * 2048 + (g * 4 + j) * 128:
                         t * 2048 + (g * 4 + j + 1) * 128] for j in range(4)]
            dsts = [o_fm[:, (g * 4 + j) * 1024 + t * 128:
                         (g * 4 + j) * 1024 + (t + 1) * 128] for j in range(4)]
            tp.group(srcs, dsts, src_ready=(s_v, o_normed))
    ofm_ready = tp.s_ev.count

    if level == 3:
        _finish_passthrough(p, nc, x_dram, xo_dram, None, s_x, s_v, s_out,
                            (tp.s_ev, ofm_ready))
        return
    kv_free_mm = main_done_mm
    kv_free_v = main_done_v
    p.wait("sync", s_mm, kv_free_mm)
    p.wait("sync", s_v, kv_free_v)
    for t in range(NTILE):
        p.dma("sync", x_sb[:, t * 2048:(t + 1) * 2048],
              x_dram[t * 128:(t + 1) * 128, :], s_x)
    x_all = s_x.count
    wo_s = Stream(p, 2, swdge=True)
    wo_rel = {0: 0, 1: 0}
    for oc in range(4):
        wb = oc % 2
        wo_loaded = wo_s.dma(
            "gpsimd", wo_sb[:, wb * 8192:(wb + 1) * 8192].rearrange(
                "p (ic f) -> p ic f", ic=16),
            wo[:, oc * 512:(oc + 1) * 512].rearrange("(ic p) f -> p ic f", p=128),
            gate=(s_mm, wo_rel[wb]), slot=wb)
        p.wait("vector", wo_loaded[0], wo_loaded[1])
        p.wait("vector", s_v, normc_ready)
        for ic in range(16):
            p.emit("vector", lambda e, o=wo_sb[:, wb * 8192 + ic * 512:
                                               wb * 8192 + (ic + 1) * 512],
                   r=normc[:, ic:ic + 1]: e.tensor_scalar(out=o, in0=o, scalar1=r,
                                                          scalar2=None, op0=ALU.mult,
                                                          op1=ALU.bypass),
                   s_v if ic == 15 else None)
        wo_scaled = s_v.count
        for t in range(NTILE):
            bank = pq[t % 2]
            p.wait("tensor", s_v, pq_rel[t % 2])
            p.wait("tensor", s_v, wo_scaled)
            if oc == 0 and t == 0:
                p.wait("tensor", tp.s_ev, ofm_ready)
            for ic in range(16):
                lhs = o_fm[:, ic * 1024 + t * 128: ic * 1024 + (t + 1) * 128]
                rhs = wo_sb[:, wb * 8192 + ic * 512: wb * 8192 + (ic + 1) * 512]
                p.emit("tensor", lambda e, o=bank, l=lhs, r=rhs,
                       st=(ic == 0), sp=(ic == 15):
                       e.matmul(o[:], l, r, start=st, stop=sp),
                       s_mm if ic == 15 else None)
            if t == NTILE - 1:
                wo_rel[wb] = s_mm.count
            prj_done = s_mm.count
            xv = x_sb[:, t * 2048 + oc * 512: t * 2048 + (oc + 1) * 512]
            p.wait("vector", s_x, x_all)
            p.wait("vector", s_mm, prj_done)
            pq_rel[t % 2] = p.emit("vector", lambda e, o=xv, i=bank:
                                   e.tensor_tensor(out=o, in0=o, in1=i[:],
                                                   op=ALU.add), s_v)
            if oc == 3:
                p.wait("scalar", s_v, s_v.count)
                p.dma("scalar", xo_dram[t * 128:(t + 1) * 128, :],
                      x_sb[:, t * 2048:(t + 1) * 2048], s_out)
    p.wait("scalar", s_out, s_out.count)


def build_ffn0_prog():
    nc = bass.Bass("TRN2", target_bir_lowering=False, debug=False)
    x = nc.dram_tensor("x", [T_CORE, D], F32, kind="ExternalInput").ap()
    wg = nc.dram_tensor("wg", [8192, D], F32, kind="ExternalInput").ap()
    wu = nc.dram_tensor("wu", [8192, D], F32, kind="ExternalInput").ap()
    wd = nc.dram_tensor("wd", [D, 8192], F32, kind="ExternalInput").ap()
    xo = nc.dram_tensor("xo", [T_CORE, D], F32, kind="ExternalOutput").ap()
    ident_d = nc.inline_tensor(np.eye(128, dtype=BF), "identc").ap()
    p = Prog(nc)
    ident_sb = nc.alloc_sbuf_tensor("ident", [128, 128], BF16)
    s_i = p.sem()
    p.dma("sync", ident_sb[:], ident_d[:], s_i)
    p.wait("tensor", s_i, 16)
    emit_ffn0(p, nc, x, xo, wg, wu, wd, ident_sb, tag="f0")
    p.build_block()
    return nc


def build_moe_prog(debug=False):
    nc = bass.Bass("TRN2", target_bir_lowering=False, debug=False)
    x = nc.dram_tensor("x", [T_CORE, D], F32, kind="ExternalInput").ap()
    W = {
        "sw1": nc.dram_tensor("sw1", [2, 2048, 1408], F32, kind="ExternalInput").ap(),
        "sb1": nc.dram_tensor("sb1", [2, 1408], F32, kind="ExternalInput").ap(),
        "sw2": nc.dram_tensor("sw2", [2, 1408, 2048], F32, kind="ExternalInput").ap(),
        "sb2": nc.dram_tensor("sb2", [2, 2048], F32, kind="ExternalInput").ap(),
        "rw1": nc.dram_tensor("rw1", [16, 2048, 1408], F32, kind="ExternalInput").ap(),
        "rb1": nc.dram_tensor("rb1", [16, 1408], F32, kind="ExternalInput").ap(),
        "rw2": nc.dram_tensor("rw2", [16, 1408, 2048], F32, kind="ExternalInput").ap(),
        "rb2": nc.dram_tensor("rb2", [16, 2048], F32, kind="ExternalInput").ap(),
        "cent": nc.dram_tensor("cent", [16, 2048], F32, kind="ExternalInput").ap(),
        "bias": nc.dram_tensor("bias", [16], F32, kind="ExternalInput").ap(),
    }
    xo = nc.dram_tensor("xo", [T_CORE, D], F32, kind="ExternalOutput").ap()
    ident_d = nc.inline_tensor(np.eye(128, dtype=BF), "identc").ap()
    ident32_d = nc.inline_tensor(np.eye(128, dtype=np.float32), "ident32c").ap()
    dbg = None
    if debug:
        dbg = {"g": nc.dram_tensor("gdbg", [T_CORE, 16], F32, kind="ExternalOutput").ap(),
               "u": nc.dram_tensor("udbg", [T_CORE, D], F32, kind="ExternalOutput").ap()}
    p = Prog(nc)
    ident_sb = nc.alloc_sbuf_tensor("ident", [128, 128], BF16)
    ident32_sb = nc.alloc_sbuf_tensor("ident32", [128, 128], F32)
    s_i = p.sem()
    p.dma("sync", ident_sb[:], ident_d[:], s_i)
    p.dma("sync", ident32_sb[:], ident32_d[:], s_i)
    p.wait("tensor", s_i, 32)
    emit_moe(p, nc, x, xo, W, ident_sb, ident32_sb[:], tag="m0", dbg=dbg)
    p.build_block()
    return nc


def shard_inputs_moe(inputs, x_full):
    names = {"sw1": "L1_ffn_shared_w1", "sb1": "L1_ffn_shared_b1",
             "sw2": "L1_ffn_shared_w2", "sb2": "L1_ffn_shared_b2",
             "rw1": "L1_ffn_routed_w1", "rb1": "L1_ffn_routed_b1",
             "rw2": "L1_ffn_routed_w2", "rb2": "L1_ffn_routed_b2",
             "cent": "L1_ffn_centroids", "bias": "L1_ffn_bias"}
    Ws = {k: np.ascontiguousarray(inputs[v]) for k, v in names.items()}
    maps = []
    for c in range(NCORES):
        a, b = core_strips(c)
        xs = np.concatenate([x_full[a * 512:(a + 1) * 512],
                             x_full[b * 512:(b + 1) * 512]], axis=0)
        maps.append({"x": xs, **Ws})
    return maps


def shard_inputs_ffn0(inputs):
    x = np.ascontiguousarray(inputs["L0_attn_x"].reshape(8192, 2048))
    wg = np.ascontiguousarray(inputs["L0_ffn_w_gate"])
    wu = np.ascontiguousarray(inputs["L0_ffn_w_up"])
    wd = np.ascontiguousarray(inputs["L0_ffn_w_down"])
    maps = []
    for c in range(NCORES):
        a, b = core_strips(c)
        xs = np.concatenate([x[a * 512:(a + 1) * 512], x[b * 512:(b + 1) * 512]], axis=0)
        maps.append({"x": xs, "wg": wg, "wu": wu, "wd": wd})
    return maps


def unshard_tokens(parts):
    out = np.empty((8192, parts[0].shape[1]), dtype=parts[0].dtype)
    for c in range(NCORES):
        a, b = core_strips(c)
        out[a * 512:(a + 1) * 512] = parts[c][:512]
        out[b * 512:(b + 1) * 512] = parts[c][512:]
    return out


def build_attn0_qkv_prog():
    nc = bass.Bass("TRN2", target_bir_lowering=False, debug=False)
    x = nc.dram_tensor("x", [T_CORE, D], F32, kind="ExternalInput").ap()
    wq = nc.dram_tensor("wq", [D, 4096], F32, kind="ExternalInput").ap()
    wk = nc.dram_tensor("wk", [D, 1024], F32, kind="ExternalInput").ap()
    wv = nc.dram_tensor("wv", [D, 512], F32, kind="ExternalInput").ap()
    cosd = nc.dram_tensor("cosd", [T_CORE, 128], F32, kind="ExternalInput").ap()
    sind = nc.dram_tensor("sind", [T_CORE, 128], F32, kind="ExternalInput").ap()
    qout = nc.dram_tensor("qout", [T_CORE, 4096], BF16, kind="ExternalOutput").ap()
    kfm = nc.dram_tensor("kfm", [1024, T_CORE], BF16, kind="ExternalOutput").ap()
    vout = nc.dram_tensor("vout", [T_CORE, 512], BF16, kind="ExternalOutput").ap()
    ident_d = nc.inline_tensor(np.eye(128, dtype=BF), "identc").ap()
    p = Prog(nc)
    ident_sb = nc.alloc_sbuf_tensor("ident", [128, 128], BF16)
    s_i = p.sem()
    p.dma("sync", ident_sb[:], ident_d[:], s_i)
    p.wait("tensor", s_i, 16)
    emit_attn0_qkv(p, nc, x, wq, wk, wv, cosd, sind, qout, kfm, vout,
                   ident_sb, tag="qk")
    p.build_block()
    return nc


def rope_tables(core):
    pos = np.array(strip_positions(core), dtype=np.float64)
    j = np.arange(128, dtype=np.float64)
    theta = 1.0 / 10000.0 ** (2.0 * j / 256.0)
    ang = pos[:, None] * theta[None, :]
    return np.cos(ang).astype(np.float32), np.sin(ang).astype(np.float32)


def strip_positions(core):
    a, b = core_strips(core)
    return list(range(a * 512, (a + 1) * 512)) + list(range(b * 512, (b + 1) * 512))


def shard_inputs_attn0_qkv(inputs, x_full):
    wq = np.ascontiguousarray(inputs["L0_attn_Wq"])
    wk = np.ascontiguousarray(inputs["L0_attn_Wk"])
    wv = np.ascontiguousarray(inputs["L0_attn_Wv"])
    maps = []
    for c in range(NCORES):
        a, b = core_strips(c)
        xs = np.concatenate([x_full[a * 512:(a + 1) * 512],
                             x_full[b * 512:(b + 1) * 512]], axis=0)
        cosd, sind = rope_tables(c)
        maps.append({"x": xs, "wq": wq, "wk": wk, "wv": wv,
                     "cosd": cosd, "sind": sind})
    return maps


def build_attn0_core_prog(core):
    raise NotImplementedError


def unshard_strip_tokens(parts):
    return unshard_tokens([np.asarray(x) for x in parts])


def attn0_masks():
    m = np.zeros((128, 4 * 512), dtype=np.float32)
    for qt in range(4):
        blk = np.full((128, 512), -1e30, dtype=np.float32)
        blk[:, :qt * 128] = 0.0
        tri = np.where(np.arange(128)[None, :] <= np.arange(128)[:, None], 0.0, -1e30)
        blk[:, qt * 128:(qt + 1) * 128] = tri
        m[:, qt * 512:(qt + 1) * 512] = blk
    return m


def build_attn0_heads_prog():
    nc = bass.Bass("TRN2", target_bir_lowering=False, debug=False)
    q = nc.dram_tensor("q", [8192, 512], BF16, kind="ExternalInput").ap()
    kg = nc.dram_tensor("kg", [256, 8192], BF16, kind="ExternalInput").ap()
    vg = nc.dram_tensor("vg", [8192, 128], BF16, kind="ExternalInput").ap()
    lq1 = nc.dram_tensor("lq1", [2, 128], F32, kind="ExternalInput").ap()
    lq2 = nc.dram_tensor("lq2", [2, 128], F32, kind="ExternalInput").ap()
    lk1 = nc.dram_tensor("lk1", [2, 128], F32, kind="ExternalInput").ap()
    lk2 = nc.dram_tensor("lk2", [2, 128], F32, kind="ExternalInput").ap()
    att = nc.dram_tensor("att", [8192, 256], BF16, kind="ExternalOutput").ap()
    masks_d = nc.inline_tensor(attn0_masks(), "masksc").ap()
    ident_d = nc.inline_tensor(np.eye(128, dtype=BF), "identc").ap()
    ident32_d = nc.inline_tensor(np.eye(128, dtype=np.float32), "ident32c").ap()
    p = Prog(nc)
    ident_sb = nc.alloc_sbuf_tensor("ident", [128, 128], BF16)
    ident32_sb = nc.alloc_sbuf_tensor("ident32", [128, 128], F32)
    s_i = p.sem()
    p.dma("sync", ident_sb[:], ident_d[:], s_i)
    p.dma("sync", ident32_sb[:], ident32_d[:], s_i)
    p.wait("tensor", s_i, 32)
    emit_attn0_heads(p, nc, q, kg, vg, lq1, lq2, lk1, lk2, att, masks_d,
                     ident_sb, ident32_sb[:], tag="ah")
    p.build_block()
    return nc


def shard_inputs_attn0_heads(inputs, q_all, k_glob, v_glob):
    maps = []
    for c in range(NCORES):
        kvh = c // 2
        maps.append({
            "q": np.ascontiguousarray(q_all[:, 2 * c * 256:(2 * c + 2) * 256]),
            "kg": np.ascontiguousarray(k_glob[kvh * 256:(kvh + 1) * 256, :]),
            "vg": np.ascontiguousarray(v_glob[:, kvh * 128:(kvh + 1) * 128]),
            "lq1": np.ascontiguousarray(inputs["L0_attn_lq1"][2 * c:2 * c + 2]),
            "lq2": np.ascontiguousarray(inputs["L0_attn_lq2"][2 * c:2 * c + 2]),
            "lk1": np.ascontiguousarray(inputs["L0_attn_lk1"][2 * c:2 * c + 2]),
            "lk2": np.ascontiguousarray(inputs["L0_attn_lk2"][2 * c:2 * c + 2]),
        })
    return maps


def build_attn0_out_prog():
    nc = bass.Bass("TRN2", target_bir_lowering=False, debug=False)
    x = nc.dram_tensor("x", [T_CORE, D], F32, kind="ExternalInput").ap()
    att = nc.dram_tensor("att", [T_CORE, D], BF16, kind="ExternalInput").ap()
    wo = nc.dram_tensor("wo", [D, D], F32, kind="ExternalInput").ap()
    rms_w = nc.dram_tensor("rms_w", [128], F32, kind="ExternalInput").ap()
    xo = nc.dram_tensor("xo", [T_CORE, D], F32, kind="ExternalOutput").ap()
    ident_d = nc.inline_tensor(np.eye(128, dtype=BF), "identc").ap()
    p = Prog(nc)
    ident_sb = nc.alloc_sbuf_tensor("ident", [128, 128], BF16)
    s_i = p.sem()
    p.dma("sync", ident_sb[:], ident_d[:], s_i)
    p.wait("tensor", s_i, 16)
    emit_attn0_out(p, nc, x, xo, att, wo, rms_w, ident_sb, tag="ao")
    p.build_block()
    return nc


def to_global_tokens(parts):
    return unshard_tokens([np.asarray(x) for x in parts])


def to_core_tokens(full, c):
    a, b = core_strips(c)
    return np.concatenate([full[a * 512:(a + 1) * 512],
                           full[b * 512:(b + 1) * 512]], axis=0)


def build_attn1_qkv_prog():
    nc = bass.Bass("TRN2", target_bir_lowering=False, debug=False)
    x = nc.dram_tensor("x", [T_CORE, D], F32, kind="ExternalInput").ap()
    wq = nc.dram_tensor("wq", [D, D], F32, kind="ExternalInput").ap()
    wk = nc.dram_tensor("wk", [D, D], F32, kind="ExternalInput").ap()
    wv = nc.dram_tensor("wv", [D, D], F32, kind="ExternalInput").ap()
    wg1 = nc.dram_tensor("wg1", [D, 128], F32, kind="ExternalInput").ap()
    wg2 = nc.dram_tensor("wg2", [128, D], F32, kind="ExternalInput").ap()
    ld = nc.dram_tensor("ld", [16], F32, kind="ExternalInput").ap()
    qo = nc.dram_tensor("qo", [T_CORE, D], BF16, kind="ExternalOutput").ap()
    ko = nc.dram_tensor("ko", [T_CORE, D], BF16, kind="ExternalOutput").ap()
    vo = nc.dram_tensor("vo", [T_CORE, D], BF16, kind="ExternalOutput").ap()
    go = nc.dram_tensor("go", [T_CORE, D], BF16, kind="ExternalOutput").ap()
    mout = nc.dram_tensor("mout", [2, 16, 128, 128], F32, kind="ExternalOutput").ap()
    colv = np.zeros((1, 512), dtype=np.float32)
    for c in range(4):
        colv[0, c * 128:(c + 1) * 128] = 511 - (c * 128 + np.arange(128))
    colv4_d = nc.inline_tensor(colv, "colv4c").ap()
    ident_d = nc.inline_tensor(np.eye(128, dtype=BF), "identc").ap()
    p = Prog(nc)
    ident_sb = nc.alloc_sbuf_tensor("ident", [128, 128], BF16)
    s_i = p.sem()
    p.dma("sync", ident_sb[:], ident_d[:], s_i)
    p.wait("tensor", s_i, 16)
    emit_attn1_qkv(p, nc, x, [wq, wk, wv], wg1, wg2, ld, qo, ko, vo, go, mout,
                   colv4_d, ident_sb, tag="a1")
    p.build_block()
    return nc


def shard_inputs_attn1_qkv(inputs, x_full):
    Ws = {"wq": np.ascontiguousarray(inputs["L1_attn_Wq"]),
          "wk": np.ascontiguousarray(inputs["L1_attn_Wk"]),
          "wv": np.ascontiguousarray(inputs["L1_attn_Wv"]),
          "wg1": np.ascontiguousarray(inputs["L1_attn_Wg1"]),
          "wg2": np.ascontiguousarray(inputs["L1_attn_Wg2"]),
          "ld": np.ascontiguousarray(inputs["L1_attn_log_decay"])}
    maps = []
    for c in range(NCORES):
        maps.append({"x": to_core_tokens(x_full, c), **Ws})
    return maps


def build_attn1_core_prog(level=5, debug=False):
    nc = bass.Bass("TRN2", target_bir_lowering=False, debug=False)
    x = nc.dram_tensor("x", [T_CORE, D], F32, kind="ExternalInput").ap()
    qd = nc.dram_tensor("qd", [T_CORE, D], BF16, kind="ExternalInput").ap()
    kd_ = nc.dram_tensor("kd", [T_CORE, D], BF16, kind="ExternalInput").ap()
    vd = nc.dram_tensor("vd", [T_CORE, D], BF16, kind="ExternalInput").ap()
    gd = nc.dram_tensor("gd", [T_CORE, D], BF16, kind="ExternalInput").ap()
    m_all = nc.dram_tensor("m_all", [16, 16, 128, 128], F32, kind="ExternalInput").ap()
    ld = nc.dram_tensor("ld", [16], F32, kind="ExternalInput").ap()
    wo = nc.dram_tensor("wo", [D, D], F32, kind="ExternalInput").ap()
    norm_w = nc.dram_tensor("norm_w", [D], F32, kind="ExternalInput").ap()
    ecoef = nc.dram_tensor("ecoef", [2, 16], F32, kind="ExternalInput").ap()
    xo = nc.dram_tensor("xo", [T_CORE, D], F32, kind="ExternalOutput").ap()
    dmat = np.zeros((128, 128), dtype=np.float32)
    for s in range(128):
        for cc in range(128):
            dmat[s, cc] = (cc - s) if cc >= s else 1e30
    dmat_d = nc.inline_tensor(dmat, "dmatc").ap()
    ip1 = np.zeros((1, 256), dtype=np.float32)
    ip1[0, 0:128] = np.arange(128) + 1
    ip1[0, 128:256] = 127 - np.arange(128)
    ip1_d = nc.inline_tensor(ip1, "ip1c").ap()
    ident_d = nc.inline_tensor(np.eye(128, dtype=BF), "identc").ap()
    ident32_d = nc.inline_tensor(np.eye(128, dtype=np.float32), "ident32c").ap()
    dbg = None
    if debug:
        dbg = {"o": nc.dram_tensor("odbg", [T_CORE, D], BF16, kind="ExternalOutput").ap(),
               "oc": nc.dram_tensor("ocdbg", [4, 128, 128], BF16, kind="ExternalOutput").ap()}
    p = Prog(nc)
    ident_sb = nc.alloc_sbuf_tensor("ident", [128, 128], BF16)
    ident32_sb = nc.alloc_sbuf_tensor("ident32", [128, 128], F32)
    s_i = p.sem()
    p.dma("sync", ident_sb[:], ident_d[:], s_i)
    p.dma("sync", ident32_sb[:], ident32_d[:], s_i)
    p.wait("tensor", s_i, 32)
    emit_attn1_core(p, nc, x, xo, qd, kd_, vd, gd, m_all, ld, wo, norm_w,
                    ecoef, dmat_d, ip1_d, ident_sb, ident32_sb[:], tag="b1",
                    level=level, dbg=dbg)
    p.build_block()
    nc._dbg_handles = getattr(p, "dbg_handles", None)
    return nc


def attn1_ecoef(core):
    e = np.zeros((2, 16), dtype=np.float32)
    for sl, s in enumerate(core_strips(core)):
        for sg in range(16):
            e[sl, sg] = 512.0 * (s - 1 - sg) if sg < s else 1e30
    return e


def shard_inputs_attn1_core(inputs, x_full, q_parts, k_parts, v_parts, g_parts,
                            m_parts):
    m_all = np.zeros((16, 16, 128, 128), dtype=np.float32)
    for c in range(NCORES):
        a, b = core_strips(c)
        m_all[a] = np.asarray(m_parts[c])[0]
        m_all[b] = np.asarray(m_parts[c])[1]
    Ws = {"ld": np.ascontiguousarray(inputs["L1_attn_log_decay"]),
          "wo": np.ascontiguousarray(inputs["L1_attn_Wo"]),
          "norm_w": np.ascontiguousarray(inputs["L1_attn_norm_w"]),
          "m_all": m_all}
    maps = []
    for c in range(NCORES):
        maps.append({"x": to_core_tokens(x_full, c),
                     "qd": np.asarray(q_parts[c]), "kd": np.asarray(k_parts[c]),
                     "vd": np.asarray(v_parts[c]), "gd": np.asarray(g_parts[c]),
                     "ecoef": attn1_ecoef(c), **Ws})
    return maps


_PROGS = {}

def _prog(name, builder):
    if name not in _PROGS:
        _PROGS[name] = builder()
    return _PROGS[name]


def kernel(**inputs):
    x = np.ascontiguousarray(np.asarray(inputs["L0_attn_x"],
                                        dtype=np.float32).reshape(8192, 2048))
    cores = list(range(NCORES))

    def run(nc, maps):
        return run_bass_kernel_spmd(nc, maps, core_ids=cores).results

    nc = _prog("ffn0", build_ffn0_prog)
    wg = np.ascontiguousarray(inputs["L0_ffn_w_gate"])
    wu = np.ascontiguousarray(inputs["L0_ffn_w_up"])
    wdn = np.ascontiguousarray(inputs["L0_ffn_w_down"])
    maps = [{"x": to_core_tokens(x, c), "wg": wg, "wu": wu, "wd": wdn}
            for c in cores]
    x = to_global_tokens([r["xo"] for r in run(nc, maps)])

    nc = _prog("a0qkv", build_attn0_qkv_prog)
    res = run(nc, shard_inputs_attn0_qkv(inputs, x))
    q_all = to_global_tokens([r["qout"] for r in res])
    v_glob = to_global_tokens([r["vout"] for r in res])
    k_glob = np.empty((1024, 8192), dtype=BF)
    for c in cores:
        a, b = core_strips(c)
        kfm = np.asarray(res[c]["kfm"])
        k_glob[:, a * 512:(a + 1) * 512] = kfm[:, 0:512]
        k_glob[:, b * 512:(b + 1) * 512] = kfm[:, 512:1024]

    nc = _prog("a0heads", build_attn0_heads_prog)
    res = run(nc, shard_inputs_attn0_heads(inputs, q_all, k_glob, v_glob))
    att_full = np.concatenate([np.asarray(r["att"]) for r in res], axis=1)

    nc = _prog("a0out", build_attn0_out_prog)
    wo0 = np.ascontiguousarray(inputs["L0_attn_Wo"])
    rmsw0 = np.ascontiguousarray(inputs["L0_attn_rms_w"])
    maps = [{"x": to_core_tokens(x, c), "att": to_core_tokens(att_full, c),
             "wo": wo0, "rms_w": rmsw0} for c in cores]
    x = to_global_tokens([r["xo"] for r in run(nc, maps)])

    nc = _prog("ffn0", build_ffn0_prog)
    maps = [{"x": to_core_tokens(x, c), "wg": wg, "wu": wu, "wd": wdn}
            for c in cores]
    x = to_global_tokens([r["xo"] for r in run(nc, maps)])

    nc = _prog("moe", build_moe_prog)
    x = to_global_tokens([r["xo"] for r in run(nc, shard_inputs_moe(inputs, x))])

    nc = _prog("a1qkv", build_attn1_qkv_prog)
    res = run(nc, shard_inputs_attn1_qkv(inputs, x))
    q_p = [np.asarray(r["qo"]) for r in res]
    k_p = [np.asarray(r["ko"]) for r in res]
    v_p = [np.asarray(r["vo"]) for r in res]
    g_p = [np.asarray(r["go"]) for r in res]
    m_p = [np.asarray(r["mout"]) for r in res]

    nc = _prog("a1core", build_attn1_core_prog)
    res = run(nc, shard_inputs_attn1_core(inputs, x, q_p, k_p, v_p, g_p, m_p))
    x = to_global_tokens([r["xo"] for r in res])

    nc = _prog("moe", build_moe_prog)
    x = to_global_tokens([r["xo"] for r in run(nc, shard_inputs_moe(inputs, x))])

    return np.ascontiguousarray(x.reshape(1, 8192, 2048))


import contextlib


def _c0(s):
    return s if s < 8 else 15 - s


def _loc(s):
    return 0 if s < 8 else 512


def _slx(s):
    return 0 if s < 8 else 1


def emit_attn0_pre(p, nc, x_dram, u_out, tag):
    s_v = p.sem(); s_sc = p.sem(); s_out = p.sem()
    lb = LNBufs(nc, tag)
    u_tok = SB(nc, f"utok{tag}", [128, 2 * 2048], BF16)
    xs = Stream(p, 2)
    us_ = Stream(p, 2)
    urel = {0: 0, 1: 0}
    for t in range(NTILE):
        buf = t % 2
        xv = lb.x[:, buf * 2048:(buf + 1) * 2048]
        xw = xs.dma("sync", xv, x_dram[t * 128:(t + 1) * 128, :], slot=buf,
                    gate=(s_v, s_v.count))
        p.wait("vector", xw[0], xw[1])
        for i in range(3):
            p.emit("vector", lambda e, o=lb.stats[:, i * 6:(i + 1) * 6],
                   x=xv[:, i * 512:(i + 1) * 512]: e.bn_stats(o, x))
        p.emit("vector", lambda e, o=lb.stats[:, 18:24], x=xv[:, 1536:2048]:
               e.bn_stats(o, x), s_v)
        p.wait("vector", s_v)
        p.emit("vector", lambda e, o=lb.mv, i=lb.stats: e.bn_aggr(o[:], i[:]), s_v)
        p.wait("vector", s_v)
        p.emit("vector", lambda e, o=lb.tmp, i=lb.mv: e.tensor_scalar(
            out=o[:], in0=i[:, 1:2], scalar1=1e-5, scalar2=None,
            op0=ALU.add, op1=ALU.bypass), s_v)
        p.wait("vector", s_v)
        p.emit("vector", lambda e, o=lb.rstd, i=lb.tmp: e.reciprocal(o[:], i[:]), s_v)
        p.wait("scalar", s_v, s_v.count)
        p.emit("scalar", lambda e, o=lb.rstd: e.activation(
            out=o[:], in_=o[:], func=AF.Sqrt), s_sc)
        p.wait("vector", s_sc, s_sc.count)
        uv = u_tok[:, buf * 2048:(buf + 1) * 2048]
        if isinstance(urel[buf], tuple):
            p.wait("vector", urel[buf][0], urel[buf][1])
        u_done = p.emit("vector", lambda e, o=uv, x=xv, m=lb.mv, r=lb.rstd:
                        e.tensor_scalar(out=o, in0=x, scalar1=m[:, 0:1], scalar2=r[:],
                                        op0=ALU.subtract, op1=ALU.mult), s_v)
        urel[buf] = us_.dma("scalar", u_out[t * 128:(t + 1) * 128, :], uv,
                            slot=buf, gate=(s_v, u_done))
    for s_ in us_.sems:
        if s_.count:
            p.wait("scalar", s_, s_.count)


def emit_attn0_res(p, nc, x_dram, rs_recv, xo_dram, tag):
    s_v = p.sem(); s_out = p.sem()
    xb = SB(nc, f"xr{tag}", [128, 2 * 2048], F32)
    rb_ = SB(nc, f"rr{tag}", [128, 2 * 2048], F32)
    xs = Stream(p, 2)
    rs = Stream(p, 2)
    os_ = Stream(p, 2)
    rel = {0: None, 1: None}
    for t in range(NTILE):
        b = t % 2
        xv = xb[:, b * 2048:(b + 1) * 2048]
        rv = rb_[:, b * 2048:(b + 1) * 2048]
        xw = xs.dma("sync", xv, x_dram[t * 128:(t + 1) * 128, :], slot=b,
                    gate=rel[b])
        rw = rs.dma("sync", rv, rs_recv[t * 128:(t + 1) * 128, :], slot=b)
        p.wait("vector", xw[0], xw[1])
        p.wait("vector", rw[0], rw[1])
        p.emit("vector", lambda e, o=xv, a=rv: e.tensor_tensor(
            out=o, in0=o, in1=a, op=ALU.add), s_v)
        rel[b] = os_.dma("scalar", xo_dram[t * 128:(t + 1) * 128, :], xv,
                         slot=b, gate=(s_v, s_v.count))
    for s_ in os_.sems:
        if s_.count:
            p.wait("scalar", s_, s_.count)


def emit_attn0_full(p, nc, u_g, wqs, wks, wvs, wos, rms_w, cosg, sing,
                    lq1, lq2, lk1, lk2, rs_send, masks_d, ident_sb,
                    ident32_sb, tag, level=5):
    s_x = p.sem(); s_v = p.sem(); s_sc = p.sem(); s_k = p.sem()
    s_mm = p.sem(); s_out = p.sem(); s_c = p.sem(); s_w = p.sem_swdge()
    tp = TP(p, ident_sb)
    k_sb = SB(nc, f"k{tag}", [128, 2 * 8192], BF16)
    v_sb = SB(nc, f"v{tag}", [128, 8192], BF16)
    wo_sb = SB(nc, f"wo{tag}", [128, 2 * 2048], BF16)
    msk_sb = SB(nc, f"msk{tag}", [128, 4 * 512], BF16)
    rmsw = SB(nc, f"rw{tag}", [128, 1], F32)
    lamv = SB(nc, f"lam{tag}", [2, 260], F32)
    lam_b = SB(nc, f"lamb{tag}", [128, 2], F32)
    lamrow = SB(nc, f"lamr{tag}", [1, 2], F32)
    ones1f = SB(nc, f"onf{tag}", [1, 128], F32)
    lqk = SB(nc, f"lqk{tag}", [2, 4 * 128], F32)
    bm20 = SB(nc, f"bm20{tag}", [128, 1], F32)
    bm2 = SB(nc, f"bm2{tag}", [128, 1], F32)
    q_h = nc.dram_tensor(f"qh{tag}", [8192, 512], BF16, kind="Internal").ap()
    pe = [PS(nc, f"pe{tag}{i}", [128, 1024], F32) for i in range(2)]
    patt = PS(nc, f"patt{tag}", [128, 512], F32)
    pq = pe[0][:, 0:512]
    pkv = pe[1][:, 0:512]
    _outer_scope = _SCOPE[0]
    _ps = contextlib.ExitStack()
    _SCOPE[0] = _ps
    u_t = SB(nc, f"ut{tag}", [128, 2 * 2048], BF16)
    u_fm_t = SB(nc, f"uf{tag}", [128, 2 * 2048], BF16)
    wproj = SB(nc, f"wp{tag}", [128, 16 * 896], BF16)
    cos_sb = SB(nc, f"cos{tag}", [128, 64 * 128], BF16)
    sin_sb = SB(nc, f"sin{tag}", [128, 64 * 128], BF16)
    qro = SB(nc, f"qro{tag}", [128, 2 * 512], BF16)
    kro = SB(nc, f"kro{tag}", [128, 2 * 256], BF16)
    t1 = SB(nc, f"t1{tag}", [128, 128], F32)
    t2 = SB(nc, f"t2{tag}", [128, 128], F32)
    t3 = SB(nc, f"t3{tag}", [128, 128], F32)
    t4 = SB(nc, f"t4{tag}", [128, 128], F32)
    _SCOPE[0] = _outer_scope

    p.emit("vector", lambda e: e.memset(ones1f[:], 1.0), s_v)
    p.emit("vector", lambda e: e.memset(bm20[:], -20.0), s_v)
    p.emit("vector", lambda e: e.memset(bm2[:], -2.0), s_v)
    bias_ready = s_v.count
    p.wait("scalar", s_v, bias_ready)
    p.dma("sync", msk_sb[:], masks_d[:], s_c)
    for i, ap in enumerate([lq1, lq2, lk1, lk2]):
        p.dma("sync", lqk[0:2, i * 128:(i + 1) * 128], ap[:], s_c)
    p.dma("sync", rmsw[:], rms_w[:, None], s_c)
    const_in = s_c.count
    p.dma("gpsimd", cos_sb[:].rearrange("p (t j) -> p t j", t=64),
          cosg.rearrange("(t p) j -> p t j", p=128), s_w)
    p.dma("gpsimd", sin_sb[:].rearrange("p (t j) -> p t j", t=64),
          sing.rearrange("(t p) j -> p t j", p=128), s_w)
    p.dma("gpsimd", wproj[:].rearrange("p (dc f) -> p dc f", dc=16)[:, :, 0:512],
          wqs.rearrange("(dc p) f -> p dc f", p=128), s_w)
    p.dma("gpsimd", wproj[:].rearrange("p (dc f) -> p dc f", dc=16)[:, :, 512:768],
          wks.rearrange("(dc p) f -> p dc f", p=128), s_w)
    p.dma("gpsimd", wproj[:].rearrange("p (dc f) -> p dc f", dc=16)[:, :, 768:896],
          wvs.rearrange("(dc p) f -> p dc f", p=128), s_w)
    p.dma("gpsimd", wo_sb[:].rearrange("p (c f) -> p c f", c=2),
          wos.rearrange("(c p) f -> p c f", p=128), s_w)
    w_loaded = s_w.count
    p.wait("vector", s_c, const_in)
    p.emit("vector", lambda e: e.tensor_scalar(out=rmsw[:], in0=rmsw[:], scalar1=0.8,
                                               scalar2=None, op0=ALU.mult,
                                               op1=ALU.bypass), s_v)
    p.wait("vector", s_v)
    p.wait("vector", s_w, w_loaded)
    p.emit("vector", lambda e: e.tensor_scalar(out=wo_sb[:], in0=wo_sb[:],
                                               scalar1=rmsw[:], scalar2=None,
                                               op0=ALU.mult, op1=ALU.bypass), s_v)
    wo_ready = s_v.count
    p.emit("vector", lambda e: e.tensor_tensor(
        out=lamv[0:2, 0:128], in0=lqk[0:2, 0:128], in1=lqk[0:2, 256:384],
        op=ALU.mult), s_v)
    p.emit("vector", lambda e: e.tensor_tensor(
        out=lamv[0:2, 128:256], in0=lqk[0:2, 128:256], in1=lqk[0:2, 384:512],
        op=ALU.mult), s_v)
    p.wait("vector", s_v)
    p.emit("vector", lambda e: e.reduce_sum(lamv[0:2, 256:257], lamv[0:2, 0:128],
                                            axis=AX.X), s_v)
    p.emit("vector", lambda e: e.reduce_sum(lamv[0:2, 257:258], lamv[0:2, 128:256],
                                            axis=AX.X), s_v)
    p.wait("scalar", s_v, s_v.count)
    p.emit("scalar", lambda e: e.activation(out=lamv[0:2, 256:258],
                                            in_=lamv[0:2, 256:258],
                                            func=AF.Exp), s_sc)
    p.wait("vector", s_sc, s_sc.count)
    p.emit("vector", lambda e: e.tensor_tensor(out=lamv[0:2, 258:259],
                                               in0=lamv[0:2, 256:257],
                                               in1=lamv[0:2, 257:258],
                                               op=ALU.subtract), s_v)
    p.wait("vector", s_v)
    p.emit("vector", lambda e: e.tensor_scalar(out=lamv[0:2, 258:259],
                                               in0=lamv[0:2, 258:259], scalar1=0.2,
                                               scalar2=None, op0=ALU.add,
                                               op1=ALU.bypass), s_v)
    lam_done = s_v.count
    p.wait("tensor", s_v, lam_done)
    p.emit("tensor", lambda e: e.transpose(patt[0:1, 0:2], lamv[0:2, 258:259],
                                           ident32_sb[0:2, 0:2]), s_mm)
    p.wait("vector", s_mm, s_mm.count)
    p.emit("vector", lambda e: e.tensor_copy(lamrow[:], patt[0:1, 0:2]), s_v)
    p.wait("tensor", s_v, s_v.count)
    p.emit("tensor", lambda e: e.matmul(patt[0:128, 0:2], ones1f[:], lamrow[:],
                                        start=True, stop=True), s_mm)
    p.wait("vector", s_mm, s_mm.count)
    p.emit("vector", lambda e: e.tensor_copy(lam_b[:], patt[0:128, 0:2]), s_v)
    lam_ready = s_v.count

    us = Stream(p, 2)
    qh_s = Stream(p, 2)
    urel = {0: 0, 1: 0}
    pq_rel = 0
    pkv_rel = 0
    qro_rel = {0: (None, 0), 1: (None, 0)}

    def rope_block(dstv, srcv, tt, nblk):
        for hh in range(nblk):
            base = hh * 256
            re = srcv[:, base:base + 256].rearrange("p (j two) -> p two j", two=2)[:, 0]
            im = srcv[:, base:base + 256].rearrange("p (j two) -> p two j", two=2)[:, 1]
            reo = dstv[:, base:base + 256].rearrange("p (j two) -> p two j", two=2)[:, 0]
            imo = dstv[:, base:base + 256].rearrange("p (j two) -> p two j", two=2)[:, 1]
            cs = cos_sb[:, tt * 128:(tt + 1) * 128]
            sn = sin_sb[:, tt * 128:(tt + 1) * 128]
            p.emit("vector", lambda e, o=t1, a=re, b=cs:
                   e.tensor_tensor(out=o[:], in0=a, in1=b, op=ALU.mult))
            p.emit("vector", lambda e, o=t3, a=re, b=sn:
                   e.tensor_tensor(out=o[:], in0=a, in1=b, op=ALU.mult))
            p.emit("vector", lambda e, o=t2, a=im, b=sn:
                   e.tensor_tensor(out=o[:], in0=a, in1=b, op=ALU.mult))
            p.emit("vector", lambda e, o=t4, a=im, b=cs:
                   e.tensor_tensor(out=o[:], in0=a, in1=b, op=ALU.mult), s_v)
            p.wait("vector", s_v)
            p.emit("vector", lambda e, o=reo, a=t1, b=t2:
                   e.tensor_tensor(out=o, in0=a[:], in1=b[:], op=ALU.subtract))
            p.emit("vector", lambda e, o=imo, a=t3, b=t4:
                   e.tensor_tensor(out=o, in0=a[:], in1=b[:], op=ALU.add), s_v)
            p.wait("vector", s_v)

    for tt in range(64):
        ub = tt % 2
        uv = u_t[:, ub * 2048:(ub + 1) * 2048]
        uw = us.dma("sync", uv, u_g[tt * 128:(tt + 1) * 128, :], slot=ub,
                    gate=(tp.s_tp, urel[ub]))
        ufv = u_fm_t[:, ub * 2048:(ub + 1) * 2048]
        for g in range(4):
            srcs = [uv[:, (g * 4 + j) * 128:(g * 4 + j + 1) * 128] for j in range(4)]
            dsts = [ufv[:, (g * 4 + j) * 128:(g * 4 + j + 1) * 128] for j in range(4)]
            tp.group(srcs, dsts, src_ready=uw)
        urel[ub] = tp.s_tp.count
        ufm_ready = tp.s_ev.count
        p.wait("tensor", tp.s_ev, ufm_ready)
        p.wait("tensor", s_w, w_loaded)
        p.wait("tensor", s_sc, pq_rel)
        for dc in range(NDC):
            lhs = ufv[:, dc * 128:(dc + 1) * 128]
            p.emit("tensor", lambda e, l=lhs, r=wproj[:, dc * 896:dc * 896 + 512],
                   st=(dc == 0), sp=(dc == 15):
                   e.matmul(pq, l, r, start=st, stop=sp),
                   s_mm if dc == 15 else None)
        q_done = s_mm.count
        p.wait("tensor", s_v, pkv_rel)
        for dc in range(NDC):
            lhs = ufv[:, dc * 128:(dc + 1) * 128]
            p.emit("tensor", lambda e, l=lhs,
                   r=wproj[:, dc * 896 + 512:dc * 896 + 896],
                   st=(dc == 0), sp=(dc == 15):
                   e.matmul(pkv[:, 0:384], l, r, start=st, stop=sp),
                   s_mm if dc == 15 else None)
        kv_done = s_mm.count
        qb = tt % 2
        p.wait("scalar", s_mm, q_done)
        if qro_rel[qb][0] is not None:
            p.wait("scalar", qro_rel[qb][0], qro_rel[qb][1])
        ev_done = p.emit("scalar", lambda e, o=qro[:, qb * 512:(qb + 1) * 512],
                         i=pq: e.activation(out=o, in_=i, func=AF.Copy), s_sc)
        pq_rel = ev_done
        p.wait("vector", s_sc, ev_done)
        if qro_rel[qb][0] is not None:
            p.wait("vector", qro_rel[qb][0], qro_rel[qb][1])
        rope_block(qro[:, qb * 512:(qb + 1) * 512], qro[:, qb * 512:(qb + 1) * 512],
                   tt, 2)
        qrope_done = s_v.count
        qro_rel[qb] = qh_s.dma("scalar", q_h[tt * 128:(tt + 1) * 128, :],
                               qro[:, qb * 512:(qb + 1) * 512], slot=qb,
                               gate=(s_v, qrope_done))

        p.wait("vector", s_mm, kv_done)
        p.emit("vector", lambda e, o=kro[:, qb * 256:(qb + 1) * 256],
               i=pkv[:, 0:256]: e.tensor_copy(o, i), s_v)
        p.emit("vector", lambda e, o=v_sb[:, tt * 128:(tt + 1) * 128],
               i=pkv[:, 256:384]: e.tensor_copy(o, i), s_v)
        pkv_rel_v = s_v.count
        p.wait("vector", s_v, pkv_rel_v)
        rope_block(kro[:, qb * 256:(qb + 1) * 256], kro[:, qb * 256:(qb + 1) * 256],
                   tt, 1)
        krope_done = s_v.count
        for cc_ in range(2):
            tp.group([kro[:, qb * 256 + cc_ * 128: qb * 256 + (cc_ + 1) * 128]],
                     [k_sb[:, cc_ * 8192 + tt * 128: cc_ * 8192 + (tt + 1) * 128]],
                     src_ready=(s_v, krope_done))
        pkv_rel = pkv_rel_v
    kv_built = tp.s_ev.count
    qh_fin = [(s_.h, s_.count) for s_ in qh_s.sems]
    if level == 0:
        for hnd, cnt in qh_fin:
            p.on("scalar", lambda e, h=hnd, v=cnt: e.wait_ge(h, v) if v else None)
        p.wait("scalar", tp.s_ev, kv_built)
        _ps.close()
        _SCOPE[0] = _outer_scope
        return
    _ps.close()
    _SCOPE[0] = _outer_scope
    p0_sb = SB(nc, f"p0{tag}", [128, 8192], BF16)
    p1_sb = SB(nc, f"p1{tag}", [128, 2 * 8192], BF16)
    att_sb = SB(nc, f"att{tag}", [128, 2 * 4 * 256], BF16)
    qst = SB(nc, f"qst{tag}", [128, 2 * 4 * 256], BF16)
    q_fm = SB(nc, f"qfm{tag}", [128, 2 * 2 * 4 * 128], BF16)
    wTb = SB(nc, f"wTb{tag}", [128, 2 * 64 * 128], BF16)
    zb = SB(nc, f"zb{tag}", [128, 2 * 8], F32)
    Z0 = SB(nc, f"Z0{tag}", [128, 1], F32)
    Z1 = SB(nc, f"Z1{tag}", [128, 1], F32)
    rz0 = SB(nc, f"rz0{tag}", [128, 1], F32)
    rz1 = SB(nc, f"rz1{tag}", [128, 1], F32)
    Z2p = SB(nc, f"Z2{tag}", [128, 2], F32)
    rz2p = SB(nc, f"rz2{tag}", [128, 2], F32)
    c1n = SB(nc, f"c1n{tag}", [128, 1], F32)
    ssb = SB(nc, f"ss{tag}", [128, 4 * 2], F32)
    rzb = SB(nc, f"rzb{tag}", [128, 4 * 2], F32)
    hsc = SB(nc, f"hsc{tag}", [128, 2], F32)
    junk = SB(nc, f"jk{tag}", [128, 128], BF16)
    att_fm = SB(nc, f"afm{tag}", [128, 2 * 128], BF16)
    rs_t = SB(nc, f"rst{tag}", [128, 2 * 2048], F32)
    patt2 = [patt, PS(nc, f"pat2{tag}", [128, 512], F32)]

    SC = 1.0 / np.sqrt(128.0)
    p.wait("scalar", s_v, pkv_rel)
    p.emit("scalar", lambda e: e.nop(), s_sc)
    pe_rel = [s_sc.count, s_sc.count]
    rs_rel = {0: None, 1: None}
    rs_s = Stream(p, 2)
    st = {"p0_rel": 0, "qst_rel": 0, "out_all_prev": []}
    exp2_of = [0, 0]
    p1_rel_par = [0, 0]
    av_done_par = [0, 0]
    patt_rel = [0, 0]
    qfm_ready_hl = [0, 0]
    witems = [(s, hl, qt) for s in range(16) for hl in range(2) for qt in range(4)]

    def stage1(wi):
        s, hl, qt = witems[wi]
        par = wi % 2
        ng = s + 1
        L = ng * 512
        NB = s * 4 + qt + 1
        p1b = p1_sb[:, par * 8192: (par + 1) * 8192]
        if qt == 0:
            p.wait("sync", tp.s_tp, st["qst_rel"])
            for hnd, cnt in qh_fin:
                if cnt:
                    p.on("sync", lambda e, h=hnd, v=cnt: e.wait_ge(h, v))
            for q4 in range(4):
                p.dma("sync", qst[:, (hl * 4 + q4) * 256:(hl * 4 + q4 + 1) * 256],
                      q_h[s * 512 + q4 * 128: s * 512 + (q4 + 1) * 128,
                          hl * 256:(hl + 1) * 256], s_x)
            q_loaded = s_x.count
            for q4 in range(4):
                for c in range(2):
                    tp.group([qst[:, (hl * 4 + q4) * 256 + c * 128:
                                  (hl * 4 + q4) * 256 + (c + 1) * 128]],
                             [q_fm[:, ((hl * 2 + c) * 4 + q4) * 128:
                                   ((hl * 2 + c) * 4 + q4 + 1) * 128]],
                             src_ready=(s_x, q_loaded))
            st["qst_rel"] = tp.s_tp.count
            qfm_ready_hl[hl] = tp.s_ev.count
        for c in range(2):
            zv = zb[:, c * 8:(c + 1) * 8]
            first_evac = True
            for gp in range(0, ng, 2):
                take = min(2, ng - gp)
                bank = pe[(gp // 2) % 2]
                p.wait("tensor", s_sc, pe_rel[(gp // 2) % 2])
                if gp == 0 and c == 0:
                    p.wait("tensor", tp.s_ev, max(qfm_ready_hl[hl], kv_built))
                for g2 in range(take):
                    g = gp + g2
                    lhs = q_fm[:, ((hl * 2 + c) * 4 + qt) * 128:
                               ((hl * 2 + c) * 4 + qt + 1) * 128]
                    rhs = k_sb[:, c * 8192 + g * 512: c * 8192 + (g + 1) * 512]
                    p.emit("tensor", lambda e, o=bank[:, g2 * 512:(g2 + 1) * 512],
                           l=lhs, r=rhs: e.matmul(o, l, r, start=True,
                                                  stop=True), s_mm)
                mm_done = s_mm.count
                if gp <= s < gp + take:
                    g2 = s - gp
                    p.wait("vector", s_mm, mm_done)
                    p.wait("vector", s_c, const_in)
                    p.emit("vector", lambda e, o=bank[:, g2 * 512:(g2 + 1) * 512],
                           m=msk_sb[:, qt * 512:(qt + 1) * 512]:
                           e.tensor_tensor(out=o, in0=o, in1=m, op=ALU.add), s_v)
                    p.wait("scalar", s_v, s_v.count)
                p.wait("scalar", s_mm, mm_done)
                if first_evac:
                    if c == 1:
                        p.wait("scalar", tp.s_tp, p1_rel_par[par])
                    else:
                        p.wait("scalar", s_v, st["p0_rel"])
                    first_evac = False
                pz = p0_sb if c == 0 else p1b
                pe_rel[(gp // 2) % 2] = p.emit(
                    "scalar", lambda e, o=pz[:, gp * 512: gp * 512 + take * 512],
                    i=bank[:, 0:take * 512], z=zv[:, gp // 2: gp // 2 + 1]:
                    e.activation(out=o, in_=i, func=AF.Exp, bias=bm20[:],
                                 scale=SC, accum_out=z), s_sc)
            p.wait("vector", s_sc, s_sc.count)
            Zc = Z0 if c == 0 else Z1
            p.emit("vector", lambda e, o=Zc, i=zv[:, 0:(ng + 1) // 2]:
                   e.reduce_sum(o[:], i, axis=AX.X), s_v)
            p.wait("vector", s_v)
            rc = rz0 if c == 0 else rz1
            p.emit("vector", lambda e, o=rc, i=Zc: e.reciprocal(o[:], i[:]), s_v)
            p.wait("vector", s_v)
        p.wait("vector", s_v, lam_ready)
        p.emit("vector", lambda e, o=c1n, a=Z0, b=rz1: e.tensor_tensor(
            out=o[:], in0=a[:], in1=b[:], op=ALU.mult), s_v)
        p.wait("vector", s_v)
        p.emit("vector", lambda e, o=c1n, i=lam_b[:, hl:hl + 1]: e.tensor_tensor(
            out=o[:], in0=o[:], in1=i, op=ALU.mult), s_v)
        p.wait("vector", s_v)
        p.emit("vector", lambda e, o=c1n: e.tensor_scalar(
            out=o[:], in0=o[:], scalar1=-1.0, scalar2=None, op0=ALU.mult,
            op1=ALU.bypass), s_v)
        p.wait("vector", s_v)
        p.emit("vector", lambda e, o=p1b[:, 0:L], a=p1b[:, 0:L],
               b=p0_sb[:, 0:L], c1=c1n: e.scalar_tensor_tensor(
                   out=o, in0=a, scalar=c1[:], in1=b, op0=ALU.mult,
                   op1=ALU.add), s_v)
        st["p0_rel"] = s_v.count
        p.wait("vector", s_v)
        p.emit("vector", lambda e, o=p1b[:, s * 512:(s + 1) * 512],
               m=msk_sb[:, qt * 512:(qt + 1) * 512]:
               e.tensor_tensor(out=o, in0=o, in1=m, op=ALU.add), s_v)
        w_done = s_v.count
        p.wait("scalar", s_v, w_done)
        exp2_done = p.emit("scalar", lambda e, o=p1b[:, 0:L], i=p1b[:, 0:L],
                           sc2=rz0, z=Z2p[:, par:par + 1]:
                           e.activation(out=o, in_=i, func=AF.Exp, bias=bm2[:],
                                        scale=sc2[:], accum_out=z), s_sc)
        p.wait("vector", s_sc, exp2_done)
        p.emit("vector", lambda e, o=rz2p[:, par:par + 1], i=Z2p[:, par:par + 1]:
               e.reciprocal(o, i), s_v)
        exp2_of[par] = exp2_done

    def stage2a(wi):
        s, hl, qt = witems[wi]
        par = wi % 2
        NB = s * 4 + qt + 1
        p1b = p1_sb[:, par * 8192: (par + 1) * 8192]
        evc = {}
        for kb in range(NB):
            sr = None
            if kb == 0:
                p.wait("tensor", s_mm, av_done_par[par])
                sr = (s_sc, exp2_of[par])
            tp.group([p1b[:, kb * 128:(kb + 1) * 128]],
                     [wTb[:, (par * 64 + kb) * 128:(par * 64 + kb + 1) * 128]],
                     src_ready=sr)
            evc[kb] = tp.s_ev.count
        p1_rel_par[par] = tp.s_tp.count
        st["evc"] = evc

    def stage2b(wi):
        s, hl, qt = witems[wi]
        par = wi % 2
        NB = s * 4 + qt + 1
        pattb = patt2[par]
        slot = hl * 4 + qt
        evc = st["evc"]
        for kb in range(NB):
            p.wait("tensor", tp.s_ev, evc[kb])
            if kb == 0:
                p.wait("tensor", s_v, patt_rel[par])
            p.emit("tensor", lambda e, o=pattb[:, 0:128],
                   l=wTb[:, (par * 64 + kb) * 128:(par * 64 + kb + 1) * 128],
                   r=v_sb[:, kb * 128:(kb + 1) * 128], st_=(kb == 0),
                   sp=(kb == NB - 1): e.matmul(o, l, r, start=st_, stop=sp),
                   s_mm if kb == NB - 1 else None)
        av_done_par[par] = s_mm.count
        p.wait("scalar", s_mm, av_done_par[par])
        p.wait("scalar", s_sc, s_sc.count)
        for _op in st["out_all_prev"]:
            p.wait("scalar", _op[0], _op[1])
        sq_done = p.emit("scalar", lambda e, o=junk, i=pattb[:, 0:128],
                         z=ssb[:, qt * 2 + hl: qt * 2 + hl + 1]:
                         e.activation(out=o[:], in_=i, func=AF.Square,
                                      accum_out=z), s_sc)
        p.wait("vector", s_sc, sq_done)
        for _op in st["out_all_prev"]:
            p.wait("vector", _op[0], _op[1])
        p.emit("vector", lambda e, o=rzb[:, qt * 2 + hl: qt * 2 + hl + 1],
               i=rz2p[:, par:par + 1]: e.tensor_copy(o, i), s_v)
        p.wait("vector", s_v)
        patt_rel[par] = p.emit("vector", lambda e,
                               o=att_sb[:, slot * 256: slot * 256 + 128],
                               i=pattb[:, 0:128], r=rz2p[:, par:par + 1]:
                               e.tensor_scalar(out=o, in0=i, scalar1=r,
                                               scalar2=None, op0=ALU.mult,
                                               op1=ALU.bypass), s_v)

    def tail(s):
        for qt in range(4):
            p.wait("vector", s_v, s_v.count)
            p.emit("vector", lambda e, o=hsc, a=ssb[:, qt * 2:(qt + 1) * 2],
                   b=rzb[:, qt * 2:(qt + 1) * 2]:
                   e.tensor_tensor(out=o[:], in0=a, in1=b, op=ALU.mult), s_v)
            p.wait("vector", s_v)
            p.emit("vector", lambda e, o=hsc, b=rzb[:, qt * 2:(qt + 1) * 2]:
                   e.tensor_tensor(out=o[:], in0=o[:], in1=b, op=ALU.mult), s_v)
            p.wait("vector", s_v)
            p.emit("vector", lambda e, o=hsc: e.tensor_scalar(
                out=o[:], in0=o[:], scalar1=1.0 / 128.0, scalar2=1e-8,
                op0=ALU.mult, op1=ALU.add), s_v)
            p.wait("scalar", s_v, s_v.count)
            p.emit("scalar", lambda e, o=hsc: e.activation(out=o[:], in_=o[:],
                                                           func=AF.Ln), s_sc)
            p.wait("scalar", s_sc, s_sc.count)
            ex_done = p.emit("scalar", lambda e, o=hsc: e.activation(
                out=o[:], in_=o[:], func=AF.Exp, scale=-0.5), s_sc)
            p.wait("vector", s_sc, ex_done)
            for hl in range(2):
                p.emit("vector", lambda e,
                       o=att_sb[:, (hl * 4 + qt) * 256:(hl * 4 + qt) * 256 + 128],
                       sc2=hsc[:, hl:hl + 1]:
                       e.tensor_scalar(out=o, in0=o, scalar1=sc2, scalar2=None,
                                       op0=ALU.mult, op1=ALU.bypass), s_v)
        att_scaled = s_v.count
        for qt in range(4):
            tp.group([att_sb[:, (0 * 4 + qt) * 256:(0 * 4 + qt) * 256 + 128],
                      att_sb[:, (1 * 4 + qt) * 256:(1 * 4 + qt) * 256 + 128]],
                     [att_fm[:, 0:128], att_fm[:, 128:256]],
                     src_ready=(s_v, att_scaled))
            afm_ready = tp.s_ev.count
            rb = qt % 2
            for oc in range(4):
                bank = pe[oc % 2]
                p.wait("tensor", s_sc, pe_rel[oc % 2])
                p.wait("tensor", tp.s_ev, afm_ready)
                p.wait("tensor", s_v, wo_ready)
                for fchunk in range(2):
                    p.emit("tensor", lambda e, o=bank[:, 0:512],
                           l=att_fm[:, fchunk * 128:(fchunk + 1) * 128],
                           r=wo_sb[:, fchunk * 2048 + oc * 512:
                                   fchunk * 2048 + (oc + 1) * 512],
                           st_=(fchunk == 0), sp=(fchunk == 1):
                           e.matmul(o, l, r, start=st_, stop=sp),
                           s_mm if fchunk == 1 else None)
                prj = s_mm.count
                p.wait("vector", s_mm, prj)
                if rs_rel[rb] is not None:
                    p.wait("vector", rs_rel[rb][0], rs_rel[rb][1])
                ev2 = p.emit("vector", lambda e,
                             o=rs_t[:, rb * 2048 + oc * 512:
                                    rb * 2048 + (oc + 1) * 512],
                             i=bank[:, 0:512]: e.tensor_copy(o, i), s_v)
                p.wait("scalar", s_v, ev2)
                p.emit("scalar", lambda e: e.nop(), s_sc)
                pe_rel[oc % 2] = s_sc.count
            rs_done = s_v.count
            r0 = _c0(s) * 1024 + _loc(s) + qt * 128
            rs_rel[rb] = rs_s.dma("scalar", rs_send[r0:r0 + 128, :],
                                  rs_t[:, rb * 2048:(rb + 1) * 2048],
                                  slot=rb, gate=(s_v, rs_done))
        st["out_all_prev"] = [(s_, s_.count) for s_ in rs_s.sems if s_.count]

    prev = None
    for wi in range(len(witems)):
        if prev is not None:
            stage2a(prev)
        stage1(wi)
        if prev is not None:
            stage2b(prev)
            ps_, phl_, pqt_ = witems[prev]
            if phl_ == 1 and pqt_ == 3:
                tail(ps_)
        prev = wi
    stage2a(prev)
    stage2b(prev)
    tail(15)
    for s_ in rs_s.sems:
        if s_.count:
            p.wait("scalar", s_, s_.count)


def build_fused_prog():
    nc = bass.Bass("TRN2", target_bir_lowering=False, debug=False)
    E = lambda n, sh, dt=F32: nc.dram_tensor(n, sh, dt, kind="ExternalInput").ap()
    I = lambda n, sh, dt: nc.dram_tensor(n, sh, dt, kind="Internal").ap()
    x_in = E("x", [T_CORE, D])
    wg0 = E("wg0", [8192, D]); wu0 = E("wu0", [8192, D]); wd0 = E("wd0", [D, 8192])
    wqs = E("wqs", [D, 512]); wks = E("wks", [D, 256]); wvs = E("wvs", [D, 128])
    wos = E("wos", [256, D]); rms_w = E("rms_w", [128])
    cosg = E("cosg", [8192, 128]); sing = E("sing", [8192, 128])
    lq1 = E("lq1", [2, 128]); lq2 = E("lq2", [2, 128])
    lk1 = E("lk1", [2, 128]); lk2 = E("lk2", [2, 128])
    W = {
        "sw1": E("sw1", [2, 2048, 1408]), "sb1": E("sb1", [2, 1408]),
        "sw2": E("sw2", [2, 1408, 2048]), "sb2": E("sb2", [2, 2048]),
        "rw1": E("rw1", [16, 2048, 1408]), "rb1": E("rb1", [16, 1408]),
        "rw2": E("rw2", [16, 1408, 2048]), "rb2": E("rb2", [16, 2048]),
        "cent": E("cent", [16, 2048]), "bias": E("bias", [16]),
    }
    wq1 = E("wq1", [D, D]); wk1 = E("wk1", [D, D]); wv1 = E("wv1", [D, D])
    wo1 = E("wo1", [D, D]); wg1 = E("wg1", [D, 128]); wg2 = E("wg2", [128, D])
    ld = E("ld", [16]); norm_w = E("norm_w", [D]); ecoef = E("ecoef", [2, 16])
    xo_ext = nc.dram_tensor("xo", [T_CORE, D], F32, kind="ExternalOutput").ap()

    xi0 = I("xi0", [T_CORE, D], F32)
    xi1 = I("xi1", [T_CORE, D], F32)
    u_loc = I("u_loc", [T_CORE, D], BF16)
    u_g = I("u_g", [8192, D], BF16)
    rs_send = I("rs_send", [8192, D], F32)
    rs_recv = I("rs_recv", [T_CORE, D], F32)
    msend = I("msend", [4096, 128], F32)
    mg = I("mg", [32768, 128], F32)
    q1d = I("q1d", [T_CORE, D], BF16); k1d = I("k1d", [T_CORE, D], BF16)
    v1d = I("v1d", [T_CORE, D], BF16); g1d = I("g1d", [T_CORE, D], BF16)

    ident_d = nc.inline_tensor(np.eye(128, dtype=BF), "identc").ap()
    ident32_d = nc.inline_tensor(np.eye(128, dtype=np.float32), "ident32c").ap()
    masks_d = nc.inline_tensor(attn0_masks().astype(BF), "masksc").ap()
    colv = np.zeros((1, 512), dtype=np.float32)
    for c in range(4):
        colv[0, c * 128:(c + 1) * 128] = 511 - (c * 128 + np.arange(128))
    colv4_d = nc.inline_tensor(colv, "colv4c").ap()
    dmat = np.zeros((128, 128), dtype=np.float32)
    for s in range(128):
        for cc_ in range(128):
            dmat[s, cc_] = (cc_ - s) if cc_ >= s else 1e30
    dmat_d = nc.inline_tensor(dmat, "dmatc").ap()
    ip1 = np.zeros((1, 256), dtype=np.float32)
    ip1[0, 0:128] = np.arange(128) + 1
    ip1[0, 128:256] = 127 - np.arange(128)
    ip1_d = nc.inline_tensor(ip1, "ip1c").ap()

    p = Prog(nc)
    ident_sb = nc.alloc_sbuf_tensor("ident", [128, 128], BF16)
    ident32_sb = nc.alloc_sbuf_tensor("ident32", [128, 128], F32)
    s_i = Sem(nc.alloc_semaphore("sident"))
    p.dma("sync", ident_sb[:], ident_d[:], s_i)
    p.dma("sync", ident32_sb[:], ident32_d[:], s_i)
    p.wait("tensor", s_i, 32)
    s_cc = Sem(nc.alloc_semaphore("scc"))

    def phase(emit_fn):
        p.sem_scope_reset()
        st = contextlib.ExitStack()
        _SCOPE[0] = st
        try:
            emit_fn()
        finally:
            _SCOPE[0] = None
            st.close()
        p.barrier()

    def collective(in_ap, out_ap, kind):
        op = ALU.add if kind in ("AllReduce", "ReduceScatter") else ALU.bypass
        p.emit("gpsimd", lambda e, i=in_ap, o=out_ap, k=kind, op_=op:
               e.collective_compute(k, op_, replica_groups=[list(range(NCORES))],
                                    ins=[i], outs=[o]), s_cc, inc=16)
        tgt = s_cc.count
        for eng in ENGINES:
            p.wait(eng, s_cc, tgt)

    phase(lambda: emit_ffn0(p, nc, x_in, xi0, wg0, wu0, wd0, ident_sb, "f0a"))
    phase(lambda: emit_attn0_pre(p, nc, xi0, u_loc, "ap"))
    collective(u_loc, u_g, "AllGather")
    phase(lambda: emit_attn0_full(p, nc, u_g, wqs, wks, wvs, wos, rms_w,
                                  cosg, sing, lq1, lq2, lk1, lk2, rs_send,
                                  masks_d, ident_sb, ident32_sb[:], "ah"))
    collective(rs_send, rs_recv, "ReduceScatter")
    phase(lambda: emit_attn0_res(p, nc, xi0, rs_recv, xi1, "ar"))
    phase(lambda: emit_ffn0(p, nc, xi1, xi0, wg0, wu0, wd0, ident_sb, "f0b"))
    phase(lambda: emit_moe(p, nc, xi0, xi1, W, ident_sb, ident32_sb[:], "m0a"))
    mout4 = msend.rearrange("(sl h pp) e -> sl h pp e", sl=2, h=16)
    phase(lambda: emit_attn1_qkv(p, nc, xi1, [wq1, wk1, wv1], wg1, wg2, ld,
                                 q1d, k1d, v1d, g1d, mout4, colv4_d, ident_sb,
                                 "a1"))
    collective(msend, mg, "AllGather")

    def mh_loads(Mh, h):
        out = []
        for sg in range(16):
            r = _c0(sg) * 4096 + _slx(sg) * 2048 + h * 128
            out.append((Mh[:, sg * 128:(sg + 1) * 128], mg[r:r + 128, :]))
        return out

    phase(lambda: emit_attn1_core(p, nc, xi1, xi0, q1d, k1d, v1d, g1d,
                                  mg.rearrange("(a b pp) e -> a b pp e", a=16, b=16),
                                  ld, wo1, norm_w, ecoef, dmat_d, ip1_d,
                                  ident_sb, ident32_sb[:], "b1",
                                  mh_loads=mh_loads))
    phase(lambda: emit_moe(p, nc, xi0, xo_ext, W, ident_sb, ident32_sb[:], "m0b"))
    p.build_block()
    return nc


def shard_inputs_fused(inputs):
    x = np.ascontiguousarray(np.asarray(inputs["L0_attn_x"],
                                        dtype=np.float32).reshape(8192, 2048))
    g = {
        "wg0": np.ascontiguousarray(inputs["L0_ffn_w_gate"]),
        "wu0": np.ascontiguousarray(inputs["L0_ffn_w_up"]),
        "wd0": np.ascontiguousarray(inputs["L0_ffn_w_down"]),
        "rms_w": np.ascontiguousarray(inputs["L0_attn_rms_w"]),
        "sw1": np.ascontiguousarray(inputs["L1_ffn_shared_w1"]),
        "sb1": np.ascontiguousarray(inputs["L1_ffn_shared_b1"]),
        "sw2": np.ascontiguousarray(inputs["L1_ffn_shared_w2"]),
        "sb2": np.ascontiguousarray(inputs["L1_ffn_shared_b2"]),
        "rw1": np.ascontiguousarray(inputs["L1_ffn_routed_w1"]),
        "rb1": np.ascontiguousarray(inputs["L1_ffn_routed_b1"]),
        "rw2": np.ascontiguousarray(inputs["L1_ffn_routed_w2"]),
        "rb2": np.ascontiguousarray(inputs["L1_ffn_routed_b2"]),
        "cent": np.ascontiguousarray(inputs["L1_ffn_centroids"]),
        "bias": np.ascontiguousarray(inputs["L1_ffn_bias"]),
        "wq1": np.ascontiguousarray(inputs["L1_attn_Wq"]),
        "wk1": np.ascontiguousarray(inputs["L1_attn_Wk"]),
        "wv1": np.ascontiguousarray(inputs["L1_attn_Wv"]),
        "wo1": np.ascontiguousarray(inputs["L1_attn_Wo"]),
        "wg1": np.ascontiguousarray(inputs["L1_attn_Wg1"]),
        "wg2": np.ascontiguousarray(inputs["L1_attn_Wg2"]),
        "ld": np.ascontiguousarray(inputs["L1_attn_log_decay"]),
        "norm_w": np.ascontiguousarray(inputs["L1_attn_norm_w"]),
    }
    pos = np.arange(8192, dtype=np.float64)
    j = np.arange(128, dtype=np.float64)
    theta = 1.0 / 10000.0 ** (2.0 * j / 256.0)
    ang = pos[:, None] * theta[None, :]
    cosg = np.cos(ang).astype(np.float32)
    sing = np.sin(ang).astype(np.float32)
    maps = []
    for c in range(NCORES):
        kvh = c // 2
        maps.append({
            "x": to_core_tokens(x, c), "cosg": cosg, "sing": sing,
            "wqs": np.ascontiguousarray(inputs["L0_attn_Wq"][:, 2 * c * 256:
                                                             (2 * c + 2) * 256]),
            "wks": np.ascontiguousarray(inputs["L0_attn_Wk"][:, kvh * 256:
                                                             (kvh + 1) * 256]),
            "wvs": np.ascontiguousarray(inputs["L0_attn_Wv"][:, kvh * 128:
                                                             (kvh + 1) * 128]),
            "wos": np.ascontiguousarray(inputs["L0_attn_Wo"][2 * c * 128:
                                                             (2 * c + 2) * 128, :]),
            "lq1": np.ascontiguousarray(inputs["L0_attn_lq1"][2 * c:2 * c + 2]),
            "lq2": np.ascontiguousarray(inputs["L0_attn_lq2"][2 * c:2 * c + 2]),
            "lk1": np.ascontiguousarray(inputs["L0_attn_lk1"][2 * c:2 * c + 2]),
            "lk2": np.ascontiguousarray(inputs["L0_attn_lk2"][2 * c:2 * c + 2]),
            "ecoef": attn1_ecoef(c), **g})
    return maps


def kernel_fused(**inputs):
    nc = build_fused_prog()
    res = run_bass_kernel_spmd(nc, shard_inputs_fused(inputs),
                               core_ids=list(range(NCORES)))
    x = to_global_tokens([r["xo"] for r in res.results])
    return np.ascontiguousarray(x.reshape(1, 8192, 2048))


_A0F_LEVEL = [5]


def build_a0full_test_prog():
    nc = bass.Bass("TRN2", target_bir_lowering=False, debug=False)
    E = lambda n, sh, dt=F32: nc.dram_tensor(n, sh, dt, kind="ExternalInput").ap()
    u_g = E("u_g", [8192, D], BF16)
    wqs = E("wqs", [D, 512]); wks = E("wks", [D, 256]); wvs = E("wvs", [D, 128])
    wos = E("wos", [256, D]); rms_w = E("rms_w", [128])
    cosg = E("cosg", [8192, 128]); sing = E("sing", [8192, 128])
    lq1 = E("lq1", [2, 128]); lq2 = E("lq2", [2, 128])
    lk1 = E("lk1", [2, 128]); lk2 = E("lk2", [2, 128])
    rs_send = nc.dram_tensor("rs_send", [8192, D], F32, kind="ExternalOutput").ap()
    ident_d = nc.inline_tensor(np.eye(128, dtype=BF), "identc").ap()
    ident32_d = nc.inline_tensor(np.eye(128, dtype=np.float32), "ident32c").ap()
    masks_d = nc.inline_tensor(attn0_masks().astype(BF), "masksc").ap()
    p = Prog(nc)
    ident_sb = nc.alloc_sbuf_tensor("ident", [128, 128], BF16)
    ident32_sb = nc.alloc_sbuf_tensor("ident32", [128, 128], F32)
    s_i = Sem(nc.alloc_semaphore("sident"))
    p.dma("sync", ident_sb[:], ident_d[:], s_i)
    p.dma("sync", ident32_sb[:], ident32_d[:], s_i)
    p.wait("tensor", s_i, 32)
    st = contextlib.ExitStack()
    _SCOPE[0] = st
    try:
        emit_attn0_full(p, nc, u_g, wqs, wks, wvs, wos, rms_w, cosg, sing,
                        lq1, lq2, lk1, lk2, rs_send, masks_d, ident_sb,
                        ident32_sb[:], "ah", level=_A0F_LEVEL[0])
    finally:
        _SCOPE[0] = None
        st.close()
    p.build_block()
    return nc


def emit_attn0_res8(p, nc, x_dram, rs_parts, xo_dram, tag):
    s_v = p.sem(); s_out = p.sem()
    xb = SB(nc, f"xr{tag}", [128, 2 * 2048], F32)
    rb_ = SB(nc, f"rr{tag}", [128, 2 * 2048], F32)
    acc = SB(nc, f"ra{tag}", [128, 2 * 2048], F32)
    xs = Stream(p, 2)
    rs = Stream(p, 2)
    os_ = Stream(p, 2)
    rel = {0: (None, 0), 1: (None, 0)}
    for t in range(NTILE):
        b = t % 2
        xv = xb[:, b * 2048:(b + 1) * 2048]
        av = acc[:, b * 2048:(b + 1) * 2048]
        xw = xs.dma("sync", xv, x_dram[t * 128:(t + 1) * 128, :], slot=b,
                    gate=rel[b] if rel[b][0] is not None else None)
        p.wait("vector", xw[0], xw[1])
        for c in range(NCORES):
            rv = rb_[:, b * 2048:(b + 1) * 2048]
            rw = rs.dma("sync", rv, rs_parts[c][t * 128:(t + 1) * 128, :], slot=b,
                        gate=(s_v, s_v.count))
            p.wait("vector", rw[0], rw[1])
            if c == 0:
                p.emit("vector", lambda e, o=av, a=rv: e.tensor_copy(o, a), s_v)
            else:
                p.emit("vector", lambda e, o=av, a=rv: e.tensor_tensor(
                    out=o, in0=o, in1=a, op=ALU.add), s_v)
            p.wait("vector", s_v)
        p.emit("vector", lambda e, o=xv, a=av: e.tensor_tensor(
            out=o, in0=o, in1=a, op=ALU.add), s_v)
        rel[b] = os_.dma("scalar", xo_dram[t * 128:(t + 1) * 128, :], xv,
                         slot=b, gate=(s_v, s_v.count))
    for s_ in os_.sems:
        if s_.count:
            p.wait("scalar", s_, s_.count)


def _phase_machinery(p):
    def phase(emit_fn):
        p.sem_scope_reset()
        st = contextlib.ExitStack()
        _SCOPE[0] = st
        try:
            emit_fn()
        finally:
            _SCOPE[0] = None
            st.close()
        p.barrier()
    return phase


def build_l1_prog():
    nc = bass.Bass("TRN2", target_bir_lowering=False, debug=False)
    E = lambda n, sh, dt=F32: nc.dram_tensor(n, sh, dt, kind="ExternalInput").ap()
    x = E("x", [T_CORE, D])
    wg0 = E("wg0", [8192, D]); wu0 = E("wu0", [8192, D]); wd0 = E("wd0", [D, 8192])
    xo1 = nc.dram_tensor("xo1", [T_CORE, D], F32, kind="ExternalOutput").ap()
    u_loc = nc.dram_tensor("u_loc", [T_CORE, D], BF16, kind="ExternalOutput").ap()
    ident_d = nc.inline_tensor(np.eye(128, dtype=BF), "identc").ap()
    p = Prog(nc)
    ident_sb = nc.alloc_sbuf_tensor("ident", [128, 128], BF16)
    s_i = Sem(nc.alloc_semaphore("sident"))
    p.dma("sync", ident_sb[:], ident_d[:], s_i)
    p.wait("tensor", s_i, 16)
    phase = _phase_machinery(p)
    phase(lambda: emit_ffn0(p, nc, x, xo1, wg0, wu0, wd0, ident_sb, "f0a"))
    phase(lambda: emit_attn0_pre(p, nc, xo1, u_loc, "ap"))
    p.build_block()
    return nc


def build_l3_prog():
    nc = bass.Bass("TRN2", target_bir_lowering=False, debug=False)
    E = lambda n, sh, dt=F32: nc.dram_tensor(n, sh, dt, kind="ExternalInput").ap()
    x = E("x", [T_CORE, D])
    rs_parts = [E(f"rs{c}", [T_CORE, D]) for c in range(NCORES)]
    wg0 = E("wg0", [8192, D]); wu0 = E("wu0", [8192, D]); wd0 = E("wd0", [D, 8192])
    W = {
        "sw1": E("sw1", [2, 2048, 1408]), "sb1": E("sb1", [2, 1408]),
        "sw2": E("sw2", [2, 1408, 2048]), "sb2": E("sb2", [2, 2048]),
        "rw1": E("rw1", [16, 2048, 1408]), "rb1": E("rb1", [16, 1408]),
        "rw2": E("rw2", [16, 1408, 2048]), "rb2": E("rb2", [16, 2048]),
        "cent": E("cent", [16, 2048]), "bias": E("bias", [16]),
    }
    wq1 = E("wq1", [D, D]); wk1 = E("wk1", [D, D]); wv1 = E("wv1", [D, D])
    wg1 = E("wg1", [D, 128]); wg2 = E("wg2", [128, D]); ld = E("ld", [16])
    xo4 = nc.dram_tensor("xo4", [T_CORE, D], F32, kind="ExternalOutput").ap()
    qo = nc.dram_tensor("qo", [T_CORE, D], BF16, kind="ExternalOutput").ap()
    ko = nc.dram_tensor("ko", [T_CORE, D], BF16, kind="ExternalOutput").ap()
    vo = nc.dram_tensor("vo", [T_CORE, D], BF16, kind="ExternalOutput").ap()
    go = nc.dram_tensor("go", [T_CORE, D], BF16, kind="ExternalOutput").ap()
    mout = nc.dram_tensor("mout", [2, 16, 128, 128], F32, kind="ExternalOutput").ap()
    colv = np.zeros((1, 512), dtype=np.float32)
    for c in range(4):
        colv[0, c * 128:(c + 1) * 128] = 511 - (c * 128 + np.arange(128))
    colv4_d = nc.inline_tensor(colv, "colv4c").ap()
    ident_d = nc.inline_tensor(np.eye(128, dtype=BF), "identc").ap()
    ident32_d = nc.inline_tensor(np.eye(128, dtype=np.float32), "ident32c").ap()
    xi2 = nc.dram_tensor("xi2", [T_CORE, D], F32, kind="Internal").ap()
    xi3 = nc.dram_tensor("xi3", [T_CORE, D], F32, kind="Internal").ap()
    p = Prog(nc)
    ident_sb = nc.alloc_sbuf_tensor("ident", [128, 128], BF16)
    ident32_sb = nc.alloc_sbuf_tensor("ident32", [128, 128], F32)
    s_i = Sem(nc.alloc_semaphore("sident"))
    p.dma("sync", ident_sb[:], ident_d[:], s_i)
    p.dma("sync", ident32_sb[:], ident32_d[:], s_i)
    p.wait("tensor", s_i, 32)
    phase = _phase_machinery(p)
    phase(lambda: emit_attn0_res8(p, nc, x, rs_parts, xi2, "ar"))
    phase(lambda: emit_ffn0(p, nc, xi2, xi3, wg0, wu0, wd0, ident_sb, "f0b"))
    phase(lambda: emit_moe(p, nc, xi3, xo4, W, ident_sb, ident32_sb[:], "m0a"))
    phase(lambda: emit_attn1_qkv(p, nc, xo4, [wq1, wk1, wv1], wg1, wg2, ld,
                                 qo, ko, vo, go, mout, colv4_d, ident_sb, "a1"))
    p.build_block()
    return nc


def build_l4_prog():
    nc = bass.Bass("TRN2", target_bir_lowering=False, debug=False)
    E = lambda n, sh, dt=F32: nc.dram_tensor(n, sh, dt, kind="ExternalInput").ap()
    x = E("x", [T_CORE, D])
    qd = E("qd", [T_CORE, D], BF16); kd_ = E("kd", [T_CORE, D], BF16)
    vd = E("vd", [T_CORE, D], BF16); gd = E("gd", [T_CORE, D], BF16)
    m_all = E("m_all", [16, 16, 128, 128])
    ld = E("ld", [16]); wo1 = E("wo1", [D, D]); norm_w = E("norm_w", [D])
    ecoef = E("ecoef", [2, 16])
    W = {
        "sw1": E("sw1", [2, 2048, 1408]), "sb1": E("sb1", [2, 1408]),
        "sw2": E("sw2", [2, 1408, 2048]), "sb2": E("sb2", [2, 2048]),
        "rw1": E("rw1", [16, 2048, 1408]), "rb1": E("rb1", [16, 1408]),
        "rw2": E("rw2", [16, 1408, 2048]), "rb2": E("rb2", [16, 2048]),
        "cent": E("cent", [16, 2048]), "bias": E("bias", [16]),
    }
    xo = nc.dram_tensor("xo", [T_CORE, D], F32, kind="ExternalOutput").ap()
    dmat = np.zeros((128, 128), dtype=np.float32)
    for s in range(128):
        for cc_ in range(128):
            dmat[s, cc_] = (cc_ - s) if cc_ >= s else 1e30
    dmat_d = nc.inline_tensor(dmat, "dmatc").ap()
    ip1 = np.zeros((1, 256), dtype=np.float32)
    ip1[0, 0:128] = np.arange(128) + 1
    ip1[0, 128:256] = 127 - np.arange(128)
    ip1_d = nc.inline_tensor(ip1, "ip1c").ap()
    ident_d = nc.inline_tensor(np.eye(128, dtype=BF), "identc").ap()
    ident32_d = nc.inline_tensor(np.eye(128, dtype=np.float32), "ident32c").ap()
    xi5 = nc.dram_tensor("xi5", [T_CORE, D], F32, kind="Internal").ap()
    p = Prog(nc)
    ident_sb = nc.alloc_sbuf_tensor("ident", [128, 128], BF16)
    ident32_sb = nc.alloc_sbuf_tensor("ident32", [128, 128], F32)
    s_i = Sem(nc.alloc_semaphore("sident"))
    p.dma("sync", ident_sb[:], ident_d[:], s_i)
    p.dma("sync", ident32_sb[:], ident32_d[:], s_i)
    p.wait("tensor", s_i, 32)
    phase = _phase_machinery(p)
    phase(lambda: emit_attn1_core(p, nc, x, xi5, qd, kd_, vd, gd, m_all, ld,
                                  wo1, norm_w, ecoef, dmat_d, ip1_d, ident_sb,
                                  ident32_sb[:], "b1"))
    phase(lambda: emit_moe(p, nc, xi5, xo, W, ident_sb, ident32_sb[:], "m0b"))
    p.build_block()
    return nc


def kernel4(**inputs):
    x = np.ascontiguousarray(np.asarray(inputs["L0_attn_x"],
                                        dtype=np.float32).reshape(8192, 2048))
    cores = list(range(NCORES))
    gmoe = {k2: np.ascontiguousarray(inputs[v]) for k2, v in {
        "sw1": "L1_ffn_shared_w1", "sb1": "L1_ffn_shared_b1",
        "sw2": "L1_ffn_shared_w2", "sb2": "L1_ffn_shared_b2",
        "rw1": "L1_ffn_routed_w1", "rb1": "L1_ffn_routed_b1",
        "rw2": "L1_ffn_routed_w2", "rb2": "L1_ffn_routed_b2",
        "cent": "L1_ffn_centroids", "bias": "L1_ffn_bias"}.items()}
    gffn = {"wg0": np.ascontiguousarray(inputs["L0_ffn_w_gate"]),
            "wu0": np.ascontiguousarray(inputs["L0_ffn_w_up"]),
            "wd0": np.ascontiguousarray(inputs["L0_ffn_w_down"])}

    def run(nc, maps):
        return run_bass_kernel_spmd(nc, maps, core_ids=cores).results

    nc = _prog("l1", build_l1_prog)
    res = run(nc, [{"x": to_core_tokens(x, c), **gffn} for c in cores])
    x1_parts = [np.asarray(r["xo1"]) for r in res]
    u_g = to_global_tokens([r["u_loc"] for r in res])

    nc = _prog("l2", build_a0full_test_prog)
    pos = np.arange(8192, dtype=np.float64)
    j = np.arange(128, dtype=np.float64)
    theta = 1.0 / 10000.0 ** (2.0 * j / 256.0)
    ang = pos[:, None] * theta[None, :]
    cosg = np.cos(ang).astype(np.float32)
    sing = np.sin(ang).astype(np.float32)
    maps = []
    for c in cores:
        kvh = c // 2
        maps.append({
            "u_g": u_g, "cosg": cosg, "sing": sing,
            "wqs": np.ascontiguousarray(inputs["L0_attn_Wq"][:, 2 * c * 256:
                                                             (2 * c + 2) * 256]),
            "wks": np.ascontiguousarray(inputs["L0_attn_Wk"][:, kvh * 256:
                                                             (kvh + 1) * 256]),
            "wvs": np.ascontiguousarray(inputs["L0_attn_Wv"][:, kvh * 128:
                                                             (kvh + 1) * 128]),
            "wos": np.ascontiguousarray(inputs["L0_attn_Wo"][2 * c * 128:
                                                             (2 * c + 2) * 128, :]),
            "rms_w": np.ascontiguousarray(inputs["L0_attn_rms_w"]),
            "lq1": np.ascontiguousarray(inputs["L0_attn_lq1"][2 * c:2 * c + 2]),
            "lq2": np.ascontiguousarray(inputs["L0_attn_lq2"][2 * c:2 * c + 2]),
            "lk1": np.ascontiguousarray(inputs["L0_attn_lk1"][2 * c:2 * c + 2]),
            "lk2": np.ascontiguousarray(inputs["L0_attn_lk2"][2 * c:2 * c + 2])})
    res = run(nc, maps)
    rs_full = [np.asarray(r["rs_send"]) for r in res]

    nc = _prog("l3", build_l3_prog)
    g1 = {"wq1": np.ascontiguousarray(inputs["L1_attn_Wq"]),
          "wk1": np.ascontiguousarray(inputs["L1_attn_Wk"]),
          "wv1": np.ascontiguousarray(inputs["L1_attn_Wv"]),
          "wg1": np.ascontiguousarray(inputs["L1_attn_Wg1"]),
          "wg2": np.ascontiguousarray(inputs["L1_attn_Wg2"]),
          "ld": np.ascontiguousarray(inputs["L1_attn_log_decay"])}
    maps = []
    for c in cores:
        m = {"x": x1_parts[c], **gffn, **gmoe, **g1}
        for src_c in cores:
            m[f"rs{src_c}"] = np.ascontiguousarray(
                rs_full[src_c][c * 1024:(c + 1) * 1024])
        maps.append(m)
    res = run(nc, maps)
    x4_parts = [np.asarray(r["xo4"]) for r in res]
    q_p = [np.asarray(r["qo"]) for r in res]
    k_p = [np.asarray(r["ko"]) for r in res]
    v_p = [np.asarray(r["vo"]) for r in res]
    g_p = [np.asarray(r["go"]) for r in res]
    m_all = np.zeros((16, 16, 128, 128), dtype=np.float32)
    for c in cores:
        a, b = core_strips(c)
        mo = np.asarray(res[c]["mout"])
        m_all[a] = mo[0]
        m_all[b] = mo[1]

    nc = _prog("l4", build_l4_prog)
    g2 = {"ld": g1["ld"], "wo1": np.ascontiguousarray(inputs["L1_attn_Wo"]),
          "norm_w": np.ascontiguousarray(inputs["L1_attn_norm_w"]),
          "m_all": m_all, **gmoe}
    maps = [{"x": x4_parts[c], "qd": q_p[c], "kd": k_p[c], "vd": v_p[c],
             "gd": g_p[c], "ecoef": attn1_ecoef(c), **g2} for c in cores]
    res = run(nc, maps)
    x6 = to_global_tokens([r["xo"] for r in res])
    return np.ascontiguousarray(x6.reshape(1, 8192, 2048))


kernel = kernel4
